# version 28
# baseline (speedup 1.0000x reference)
import math
import os

import numpy as np
import ml_dtypes
import concourse.bass as bass
import concourse.mybir as mybir
from concourse.bass_utils import run_bass_kernel_spmd

F32 = mybir.dt.float32
BF16 = mybir.dt.bfloat16
AF = mybir.ActivationFunctionType
ALU = mybir.AluOpType
AXX = mybir.AxisListType.X

T = 2048
D = 1024
NT = 16
NDC = 8
NTC = 4
DFF = 4096
RANK = 256

EPS_RMS = 2.220446049250313e-16 if os.environ.get("BASS_SIM_FP64") == "1" else 1.1920929e-07
EPS_LN = 1e-5
LAMBDA_INIT = 0.8 - 0.6 * math.exp(-0.3 * 1)


class _Buf:
    __slots__ = ("w", "r")

    def __init__(self):
        self.w = {}
        self.r = {}


def _seed(dst, *srcs):
    for s in srcs:
        for k, v in s.w.items():
            dst.r[k] = max(dst.r.get(k, 0), v)
        for k, v in s.r.items():
            dst.r[k] = max(dst.r.get(k, 0), v)


class _KB:

    def __init__(self, nc):
        self.nc = nc
        self.steps = {e: [] for e in ("pe", "act", "dve", "pool", "sp")}
        self.semh = {}
        self.cnt = {}
        names = ["cPE", "cACT", "cDVE", "cPOOL"]
        names += [f"dGP{i}" for i in range(10)] + [f"dSP{i}" for i in range(4)]
        for s in names:
            self.semh[s] = nc.alloc_semaphore(s)
            self.cnt[s] = 0
        self.clk = {"pe": "cPE", "act": "cACT", "dve": "cDVE", "pool": "cPOOL"}
        self.rr = {"dGP": 0, "dSP": 0}
        self.npool = {"dGP": 10, "dSP": 4}

    def _emit(self, eng, fn, amount, sem, reads, writes, extra):
        waits = list(extra)
        for b in reads:
            waits += b.w.items()
        for b in writes:
            waits += b.w.items()
            waits += b.r.items()
        own = self.clk.get(eng) if eng == "pe" else None
        w = {}
        for s, v in waits:
            if s is None or s == own:
                continue
            w[s] = max(w.get(s, 0), v)
        self.cnt[sem] += amount
        ts = (sem, self.cnt[sem])
        self.steps[eng].append((w, fn, ts, amount))
        for b in reads:
            b.r[ts[0]] = max(b.r.get(ts[0], 0), ts[1])
        for b in writes:
            b.w = {ts[0]: ts[1]}
            b.r = {}
        return ts

    def op(self, eng, fn, reads=(), writes=(), extra=()):
        return self._emit(eng, fn, 1, self.clk[eng], reads, writes, extra)

    def dma(self, eng, queue, fn, reads=(), writes=(), extra=()):
        sem = f"{queue}{self.rr[queue]}"
        self.rr[queue] = (self.rr[queue] + 1) % self.npool[queue]
        extra = list(extra) + [(sem, self.cnt[sem])]
        return self._emit(eng, fn, 16, sem, reads, writes, extra)

    def replay(self, eng_key, eng_obj):
        hw = {}
        for w, fn, ts, amount in self.steps[eng_key]:
            for s, v in sorted(w.items()):
                if hw.get(s, 0) < v:
                    eng_obj.wait_ge(self.semh[s], v)
                    hw[s] = v
            ins = fn(eng_obj)
            ins.then_inc(self.semh[ts[0]], amount)


def build_core(debug_stop=None):
    nc = bass.Bass("TRN2", target_bir_lowering=False)

    d_xT = nc.dram_tensor("xT", [D, T], F32, kind="ExternalInput").ap()
    d_x1 = nc.dram_tensor("x1p", [128, NT * 8 * 128], F32, kind="ExternalInput").ap()
    d_wq = nc.dram_tensor("wqT", [D, D], F32, kind="ExternalInput").ap()
    d_wk = nc.dram_tensor("wkT", [D, D], F32, kind="ExternalInput").ap()
    d_wv = nc.dram_tensor("wvT", [D, D], F32, kind="ExternalInput").ap()
    d_wp = nc.dram_tensor("wpT", [D, D], F32, kind="ExternalInput").ap()
    d_u1 = nc.dram_tensor("u1", [D, RANK], F32, kind="ExternalInput").ap()
    d_v1 = nc.dram_tensor("v1f", [RANK, DFF], F32, kind="ExternalInput").ap()
    d_u2 = nc.dram_tensor("u2", [DFF, RANK], F32, kind="ExternalInput").ap()
    d_v2 = nc.dram_tensor("v2f", [RANK, D], F32, kind="ExternalInput").ap()
    d_b1 = nc.dram_tensor("b1c", [128, 32], F32, kind="ExternalInput").ap()
    d_b2 = nc.dram_tensor("b2c", [128, 8], F32, kind="ExternalInput").ap()
    d_lam = nc.dram_tensor("lamr", [128, 128], F32, kind="ExternalInput").ap()
    d_abt = nc.dram_tensor("abtT", [128, NT * 128], F32, kind="ExternalInput").ap()
    d_mask = nc.dram_tensor("maskb", [128, 128], BF16, kind="ExternalInput").ap()
    d_ident = nc.dram_tensor("identb", [128, 128], BF16, kind="ExternalInput").ap()
    d_out = nc.dram_tensor("outT", [D, T], F32, kind="ExternalOutput").ap()

    off = [16512]
    addr = {}

    def place(name, shape, dtype, at=None):
        nbytes = int(np.prod(shape[1:])) * (2 if dtype == BF16 else 4)
        o = at if at is not None else off[0]
        h = nc.alloc_sbuf_tensor_at(name, shape, dtype, offset=o)
        addr[name] = o
        if at is None:
            off[0] += (nbytes + 63) & ~63
        return h

    XSTG = [place(f"xstg{i}", [128, 8 * 128], BF16) for i in range(4)]
    QK = [place(f"qk{i}", [128, T], BF16) for i in range(16)]
    VSB = place("vsb", [128, NT * 1024], BF16)
    YEX = [place(f"yex{i}", [128, T], BF16) for i in range(2)]
    QF = place("qf", [128, 1024], F32)
    SQ = place("sq", [128, 1024], F32)
    ROT = place("rot", [128, 2048], BF16)
    WQ = place("wq_sb", [128, 8 * 1024], BF16)
    WK = place("wk_sb", [128, 8 * 1024], BF16)
    WV = place("wv_sb", [128, 8 * 1024], BF16)
    WP = place("wp_sb", [128, 8 * 1024], BF16)
    ESA = place("esa", [128, 2048], BF16)
    BSB = place("bsb", [128, 512], F32)
    PQS = place("pqs", [128, 2048], BF16)
    USC = place("usc", [128, 512], F32)
    TTS = place("tts_sb", [128, 16 * 96], F32)
    MSK = place("mask_sb", [128, 128], BF16)
    IDT = place("ident_sb", [128, 128], BF16)
    ONES = place("ones_sb", [128, 128], BF16)
    B1C = place("b1c_sb", [128, 32], F32)
    B2C = place("b2c_sb", [128, 8], F32)
    LAMR = place("lamr_sb", [128, 128], F32)
    LSC = place("lsc", [128, 8], F32)
    MS = place("ms", [128, 16], F32)
    RS = place("rs", [128, 16], F32)
    MS2 = place("ms2", [128, 16], F32)
    RS2 = place("rs2", [128, 16], F32)
    assert off[0] <= 229344, off[0]
    MT = place("mt", [128, 512], F32, at=addr["tts_sb"])
    RT = place("rt", [128, 512], F32, at=addr["tts_sb"] + 2048)
    MT2 = place("mt2", [128, 512], F32, at=addr["tts_sb"] + 4096)
    RT2 = place("rt2", [128, 512], F32, at=addr["xstg2"])
    YR = [[place("yr00", [128, 512], F32, at=addr["tts_sb"]),
           place("yr01", [128, 512], F32, at=addr["tts_sb"] + 2048)],
          [place("yr10", [128, 512], F32, at=addr["tts_sb"] + 4096),
           place("yr11", [128, 512], F32, at=addr["xstg2"])]]
    DR = [[place("dr00", [128, 512], F32, at=addr["qf"]),
           place("dr01", [128, 512], F32, at=addr["qf"] + 2048)],
          [place("dr10", [128, 512], F32, at=addr["qf"] + 4096),
           place("dr11", [128, 512], F32, at=addr["qf"] + 6144)]]

    QB = place("qbn", [128, 1024], BF16, at=addr["yex0"])
    KB_ = place("kbn", [128, 1024], BF16, at=addr["yex0"] + 2048)
    SQK = place("sqk", [128, 1024], F32, at=addr["yex1"])
    QNS = [place("qns0", [128, 1024], BF16, at=addr["esa"]),
           place("qns1", [128, 1024], BF16, at=addr["esa"] + 2048)]
    WBS = [place("wbs0", [128, 1024], BF16, at=addr["pqs"]),
           place("wbs1", [128, 1024], BF16, at=addr["pqs"] + 2048)]
    ABT = place("abt_sb", [128, NT * 128], BF16, at=addr["rot"])
    RSE = [place("rse0", [128, 1024], BF16, at=addr["sq"]),
           place("rse1", [128, 1024], BF16, at=addr["sqk"])]
    X0T = []
    for i in range(8):
        at = addr["qf"] + 2048 * i if i < 6 else addr["xstg0"] + 2048 * (i - 6)
        X0T.append(place(f"x0t{i}", [128, 512], F32, at=at))
    X1B = [place(f"x1b{c}", [128, 512], BF16,
                 at=addr["qk3" if c < 4 else "qk7"] + 1024 * (c % 4)) for c in range(8)]
    _h_homes = ["qk0", "qk1", "qk2", "qk4", "qk5", "qk6", "yex0", "yex1"]
    HT = [place(f"ht{f}", [128, 512], BF16,
                at=addr[_h_homes[f // 4]] + 1024 * (f % 4)) for f in range(32)]
    ZX = [place(f"zx{c}", [128, T], F32,
                at=(addr["vsb"] + 8192 * c) if c < 4 else (addr[f"qk{8 + 2 * (c - 4)}"]))
          for c in range(8)]
    ZBS = [place(f"zbs{i}", [128, 512], BF16, at=addr["esa"] + 1024 * i)
           for i in range(4)]
    U1S = place("u1_sb", [128, 8 * 256], BF16, at=addr["wq_sb"])
    V2S = place("v2_sb", [128, 2 * 1024], BF16, at=addr["wq_sb"] + 4096)
    V1S = place("v1_sb", [128, 2 * 4096], BF16, at=addr["wk_sb"])
    U2S = place("u2_sb", [128, 32 * 256], BF16, at=addr["wv_sb"])

    PSA = nc.alloc_psum_tensor("psA", [128, 1024], F32)
    PSB = nc.alloc_psum_tensor("psB", [128, 1024], F32)
    PS4 = nc.alloc_psum_tensor("ps4", [128, 512], F32)
    PS5 = nc.alloc_psum_tensor("ps5", [128, 512], F32)
    PS6 = nc.alloc_psum_tensor("ps6", [128, 512], F32)
    PS7 = nc.alloc_psum_tensor("ps7", [128, 512], F32)

    kb = _KB(nc)

    bXS = [_Buf() for _ in range(4)]
    bQK = [_Buf() for _ in range(16)]
    bV = [_Buf() for _ in range(16)]
    bYEX = [_Buf() for _ in range(2)]
    bQF, bSQ, bQB, bKB, bSQK = _Buf(), _Buf(), _Buf(), _Buf(), _Buf()
    bQNS = [_Buf(), _Buf()]
    bWBS = [_Buf(), _Buf()]
    bABT = _Buf()
    bW = {k: _Buf() for k in ("u1", "v1", "u2", "v2")}
    bWc = {k: [_Buf() for _ in range(4)] for k in ("wq", "wk", "wv", "wp")}
    bEP = [[_Buf(), _Buf()], [_Buf(), _Buf()]]
    bBSB, bUSC = _Buf(), _Buf()
    bPQS = [_Buf() for _ in range(4)]
    bCONST = _Buf()
    bLSC = _Buf()
    bMS, bRS, bMS2, bRS2, bMT, bRT, bMT2, bRT2 = (_Buf() for _ in range(8))
    bYR = [[_Buf(), _Buf()], [_Buf(), _Buf()]]
    bDR = [[_Buf(), _Buf()], [_Buf(), _Buf()]]
    bPSA, bPSB, bPS4, bPS5, bPS6, bPS7 = (_Buf() for _ in range(6))
    bX0 = [_Buf() for _ in range(8)]
    bHT = [_Buf() for _ in range(32)]
    bZX = [_Buf() for _ in range(8)]
    bX1B = [_Buf() for _ in range(8)]
    bZBS = [_Buf() for _ in range(4)]

    def ld(dst_ap, src_ap):
        return lambda g: g.dma_start(out=dst_ap, in_=src_ap)

    def wload_chunk(dst, src, n, ch, bufs, extra=()):
        dv = dst[:, :].rearrange("p (c n) -> p c n", n=n)[:, 2 * ch:2 * ch + 2, :]
        sv = src.rearrange("(c p) n -> p c n", p=128)[:, 2 * ch:2 * ch + 2, :]
        kb.dma("pool", "dGP", lambda g, d=dv, s=sv: g.dma_start(out=d, in_=s),
               writes=[bufs[ch]], extra=extra)

    for ch in range(4):
        wload_chunk(WQ, d_wq, 1024, ch, bWc["wq"])
    kb.dma("pool", "dGP", ld(ABT[:, :], d_abt), writes=[bABT])
    kb.dma("sp", "dSP", ld(MSK[:, :], d_mask), writes=[bCONST])
    kb.dma("sp", "dSP", ld(IDT[:, :], d_ident), writes=[bCONST])
    kb.dma("sp", "dSP", ld(B1C[:, :], d_b1), writes=[bCONST])
    kb.dma("sp", "dSP", ld(B2C[:, :], d_b2), writes=[bCONST])
    ts_lam = kb.dma("sp", "dSP", ld(LAMR[:, :], d_lam), writes=[bCONST])

    wpend = [(WK, d_wk, ch, bWc["wk"]) for ch in range(4)]
    wpend += [(WV, d_wv, ch, bWc["wv"]) for ch in range(4)]
    wpend += [(WP, d_wp, ch, bWc["wp"]) for ch in range(4)]

    kb.op("dve", lambda v: v.memset(ONES[:, :], 1.0), writes=[bCONST])
    kb.op("dve", lambda v: v.memset(LSC[:, 3:4], float(EPS_RMS)), writes=[bLSC])
    kb.op("dve", lambda v: v.memset(LSC[:, 4:5], float(EPS_LN)), writes=[bLSC])

    kb.op("dve", lambda v: v.tensor_mul(LAMR[:, 0:32], LAMR[:, 0:32], LAMR[:, 32:64]),
          reads=[bCONST], writes=[bCONST], extra=[ts_lam])
    kb.op("dve", lambda v: v.tensor_mul(LAMR[:, 64:96], LAMR[:, 64:96], LAMR[:, 96:128]),
          reads=[bCONST], writes=[bCONST])
    kb.op("dve", lambda v: v.reduce_sum(LSC[:, 0:1], LAMR[:, 0:32], axis=AXX),
          reads=[bCONST], writes=[bLSC])
    kb.op("dve", lambda v: v.reduce_sum(LSC[:, 1:2], LAMR[:, 64:96], axis=AXX),
          reads=[bCONST], writes=[bLSC])
    kb.op("act", lambda s: s.activation(LSC[:, 0:1], LSC[:, 0:1], AF.Exp),
          reads=[bLSC], writes=[bLSC])
    kb.op("act", lambda s: s.activation(LSC[:, 1:2], LSC[:, 1:2], AF.Exp),
          reads=[bLSC], writes=[bLSC])
    kb.op("dve", lambda v: v.tensor_sub(LSC[:, 2:3], LSC[:, 0:1], LSC[:, 1:2]),
          reads=[bLSC], writes=[bLSC])
    kb.op("dve", lambda v: v.tensor_scalar(LSC[:, 2:3], LSC[:, 2:3],
                                           float(LAMBDA_INIT), -1.0, ALU.add, ALU.mult),
          reads=[bLSC], writes=[bLSC])

    def x_load(vt):
        i = vt % NT
        sl = vt % 4
        src = d_x1[:, i * 1024:(i + 1) * 1024]
        dst = XSTG[sl][:, :]
        kb.dma("pool", "dGP", lambda g, s=src, d=dst: g.dma_start(out=d, in_=s),
               writes=[bXS[sl]])
        if vt % 4 == 1 and wpend:
            dst_w, src_w, ch, bufs = wpend.pop(0)
            wload_chunk(dst_w, src_w, 1024, ch, bufs)

    for vt in range(3):
        x_load(vt)

    def qrf(i):
        return QK[8 + i // 2][:, (i % 2) * 1024:(i % 2 + 1) * 1024]

    def krf(i):
        return VSB[:, i * 1024:(i + 1) * 1024]

    tcount = [0]

    def emit_T(i, rfv, brf, slot0):
        _rot = ((PS6, bPS6), (PS7, bPS7), (PS4, bPS4), (PS5, bPS5))
        for fj in range(8):
            n = tcount[0]
            tcount[0] += 1
            ps, bps = _rot[n % 4]
            blk = rfv[:, fj * 128:(fj + 1) * 128]
            kb.op("pe", lambda t, o=ps[:, 0:128], l=blk:
                  t.matmul(o, l, IDT[:, :], start=True, stop=True, tile_position=(0, 0)),
                  reads=[brf, bCONST], writes=[bps])
            dst = QK[slot0 + fj][:, i * 128:(i + 1) * 128]
            if n % 2 == 0:
                kb.op("dve", lambda v, d=dst, s=ps[:, 0:128]: v.tensor_copy(d, s),
                      reads=[bps], writes=[bQK[slot0 + fj]])
            else:
                kb.op("act", lambda s_, d=dst, s=ps[:, 0:128]: s_.copy(d, s),
                      reads=[bps], writes=[bQK[slot0 + fj]])

    PSQv = ((PSA, bPSA), (PSB, bPSB))
    QBv, bQBv = (QB, KB_), (bQB, bKB)
    SQv, bSQv = (SQ, SQK), (bSQ, bSQK)
    MSv, bMSv = (MS, MS2), (bMS, bMS2)

    def emit_mms(vt, wsb, bwks):
        i, sl, par = vt % NT, vt % 4, vt % 2
        psq, bpsq = PSQv[par]
        for dc in range(NDC):
            lhs = XSTG[sl][:, dc * 128:(dc + 1) * 128]
            for half in (0, 1):
                rhs = wsb[:, dc * 1024 + half * 512: dc * 1024 + (half + 1) * 512]
                kb.op("pe", lambda t, o=psq[:, half * 512:(half + 1) * 512], l=lhs,
                      r=rhs, s=(dc == 0), e=(dc == NDC - 1):
                      t.matmul(o, l, r, start=s, stop=e, tile_position=(0, 0)),
                      reads=[bXS[sl], bwks[dc // 2]], writes=[bpsq])
        return psq, bpsq

    def emit_qk_pass(pidx, wsb, bwks, rf_of, brf_of, tp_of):
        for i in range(NT):
            par = i % 2
            psq, bpsq = emit_mms(pidx * NT + i, wsb, bwks)
            tpb = tp_of(i)
            if tpb is not None:
                emit_T(*tpb)
            qb, bqb = QBv[par], bQBv[par]
            sq, bsq = SQv[par], bSQv[par]
            ms, bms = MSv[par], bMSv[par]
            qns, bqns = QNS[par], bQNS[par]
            wb, bwb = WBS[par], bWBS[par]
            rse, brse = RSE[par], bSQv[par]
            rfv, brf = rf_of(i), brf_of(i)
            kb.op("act", lambda s, d=qb[:, :], p=psq[:, :]: s.copy(d, p),
                  reads=[bpsq], writes=[bqb])
            kb.op("act", lambda s, d=sq[:, :], p=psq[:, :]:
                  s.activation(d, p, AF.Square), reads=[bpsq], writes=[bsq])
            qv = qb[:, :].rearrange("p (h s c) -> p h s c", s=2, c=32)
            sv = qns[:, :].rearrange("p (h s c) -> p h s c", s=2, c=32)
            kb.op("pool", lambda g, d=sv[:, :, 0, :], s=qv[:, :, 1, :]:
                  g.tensor_copy(d, s), reads=[bqb], writes=[bqns])
            kb.op("pool", lambda g, d=sv[:, :, 1, :], s=qv[:, :, 0, :]:
                  g.tensor_copy(d, s), reads=[bqb], writes=[bqns])
            kb.op("dve", lambda v, o=ms[:, :], s=sq[:, :]:
                  v.reduce_sum(o, s.rearrange("p (s c) -> p s c", c=64), axis=AXX),
                  reads=[bsq], writes=[bms])
            kb.op("act", lambda s, m=ms[:, :]: s.activation(
                m, m, AF.Ln, bias=LSC[:, 3:4], scale=1.0 / 64.0),
                reads=[bms, bLSC], writes=[bms])
            rsev = rse[:, :].rearrange("p (s c) -> p s c", c=64)
            msb = ms[:, :].unsqueeze(2).broadcast_to([128, 16, 64])
            kb.op("act", lambda s, r=rsev, m=msb: s.activation(r, m, AF.Exp, scale=-0.5),
                  reads=[bms, bsq], writes=[brse])
            qn3 = qb[:, :].rearrange("p (s c) -> p s c", c=64)
            qs3 = qns[:, :].rearrange("p (s c) -> p s c", c=64)
            wb3 = wb[:, :].rearrange("p (s c) -> p s c", c=64)
            rf3 = rfv.rearrange("p (s c) -> p s c", c=64)
            ca = ABT[:, i * 128: i * 128 + 64].unsqueeze(1).broadcast_to([128, 16, 64])
            sa = ABT[:, i * 128 + 64: i * 128 + 128].unsqueeze(1).broadcast_to([128, 16, 64])
            kb.op("dve", lambda v, o=rf3, q=qn3, m=ca: v.tensor_mul(o, q, m),
                  reads=[bqb, bABT], writes=[brf])
            kb.op("dve", lambda v, o=wb3, q=qs3, m=sa: v.tensor_mul(o, q, m),
                  reads=[bqns, bABT], writes=[bwb])
            kb.op("dve", lambda v, o=qns[:, :], a=rfv, b=wb[:, :]: v.tensor_add(o, a, b),
                  reads=[bwb, brf, bqns], writes=[bqns])
            kb.op("dve", lambda v, o=rfv, a=qns[:, :], r=rse[:, :]: v.tensor_mul(o, a, r),
                  reads=[brse, bqns, brf], writes=[brf])
            if pidx * NT + i + 3 < 48:
                x_load(pidx * NT + i + 3)

    emit_qk_pass(0, WQ, bWc["wq"], qrf, lambda i: bQK[8 + i // 2], lambda i: None)
    emit_qk_pass(1, WK, bWc["wk"], krf, lambda i: bV[i],
                 lambda i: (i, qrf(i), bQK[8 + i // 2], 0))
    for i in range(NT):
        emit_T(i, krf(i), bV[i], 8)
        psq, bpsq = emit_mms(2 * NT + i, WV, bWc["wv"])
        kb.op("act", lambda s, d=VSB[:, i * 1024:(i + 1) * 1024], p=psq[:, :]:
              s.copy(d, p), reads=[bpsq], writes=[bV[i]])
        if 2 * NT + i + 3 < 48:
            x_load(2 * NT + i + 3)

    if debug_stop == "qkgen":
        return _finish_debug(nc, kb, QK, VSB, None, bQK, bV, bYEX)

    def warena(*keys):
        ex = []
        for k in keys:
            for b in bWc[k]:
                ex += list(b.w.items()) + list(b.r.items())
        return ex

    def wload(dst, src, n):
        return lambda g: g.dma_start(
            out=dst[:, :].rearrange("p (c n) -> p c n", n=n),
            in_=src.rearrange("(c p) n -> p c n", p=128))

    kb.dma("pool", "dGP", wload(V1S, d_v1, 4096), writes=[bW["v1"]],
           extra=warena("wk"))
    kb.dma("pool", "dGP", wload(U2S, d_u2, 256), writes=[bW["u2"]],
           extra=warena("wv"))
    kb.dma("pool", "dGP", wload(U1S, d_u1, 256), writes=[bW["u1"]],
           extra=warena("wq"))
    kb.dma("pool", "dGP", wload(V2S, d_v2, 1024), writes=[bW["v2"]],
           extra=warena("wq"))

    ycomb = {0: (YEX[0], bYEX[0]), 1: (YEX[1], bYEX[1]),
             2: (QK[0], bQK[0]), 3: (QK[4], bQK[4]),
             4: (QK[1], bQK[1]), 5: (QK[5], bQK[5]),
             6: (QK[2], bQK[2]), 7: (QK[6], bQK[6])}
    EA2 = place("ea2", [128, 1024], BF16, at=addr["xstg0"])
    EB2 = place("eb2", [128, 1024], BF16, at=addr["rot"])
    EP = {0: (ESA[:, 0:1024], ESA[:, 1024:2048], EA2[:, :]),
          1: (PQS[:, 0:1024], PQS[:, 1024:2048], EB2[:, :])}
    for j in range(2):
        _seed(bEP[0][j], bQNS[j])
        _seed(bEP[1][j], bWBS[j])
    bEP[0].append(_Buf())
    bEP[1].append(_Buf())
    _seed(bEP[0][2], bXS[0], bXS[1])
    _seed(bEP[1][2], bABT)
    _seed(bYEX[0], bQB)
    _seed(bYEX[1], bKB)
    _seed(bYEX[0], bKB)
    _seed(bYEX[1], bSQK)

    _seed(bYR[0][0], bCONST)
    _seed(bYR[0][1], bCONST)
    _seed(bYR[1][0], bXS[2])
    _seed(bYR[1][1], bXS[3])
    _seed(bDR[0][0], bQF)
    _seed(bDR[0][1], bQF)
    _seed(bDR[1][0], bSQ)
    _seed(bDR[1][1], bSQ)

    SPS = {0: (PSA, bPSA), 1: (PSB, bPSB)}
    YPS = {0: (PS4, bPS4), 1: (PS5, bPS5)}
    DPS = {0: (PS6, bPS6), 1: (PS7, bPS7)}

    pending = []

    def flush_pending():
        for eng, fn, reads, writes in pending:
            kb.op(eng, fn, reads=reads, writes=writes)
        pending.clear()

    for h in range(8):
        p = h // 2
        pbase = 64 * (h % 2)

        def emit_avden(pr, tc, h=h):
            st_last = 4 * (tc + 1) - 1
            for g in (0, 1):
                yps, byps = YPS[g]
                dps, bdps = DPS[g]
                ep, bep = EP[g][pr % 3], bEP[g][pr % 3]
                for k_ in (0, 1):
                    st = 2 * pr + k_
                    e = ep[:, k_ * 512:(k_ + 1) * 512]
                    vsl = VSB[:, st * 1024 + h * 128: st * 1024 + (h + 1) * 128]
                    kb.op("pe", lambda t, o=yps[:, :], l=vsl, e_=e,
                          s=(st == 0), en=(st == st_last):
                          t.matmul(o, l, e_, start=s, stop=en, tile_position=(0, 0)),
                          reads=[bV[st], bep], writes=[byps])
                    kb.op("pe", lambda t, o=dps[:, :], e_=e,
                          s=(st == 0), en=(st == st_last):
                          t.matmul(o, ONES[:, :], e_, start=s, stop=en,
                                   tile_position=(0, 0)),
                          reads=[bCONST, bep], writes=[bdps])

        for tc in range(NTC):
            npr = 2 * (tc + 1)
            for pr in range(npr):
                st0 = 2 * pr
                for g in (0, 1):
                    sps, bsps = SPS[g]
                    kslot, qslot = 8 + g * 4 + p, g * 4 + p
                    for k_ in (0, 1):
                        st = st0 + k_
                        lhs = QK[kslot][pbase:pbase + 64, st * 128:(st + 1) * 128]
                        rhs = QK[qslot][pbase:pbase + 64, tc * 512:(tc + 1) * 512]
                        kb.op("pe", lambda t, o=sps[:, k_ * 512:(k_ + 1) * 512],
                              l=lhs, r=rhs, tp=(pbase, 0):
                              t.matmul(o, l, r, start=True, stop=True, tile_position=tp),
                              reads=[bQK[kslot], bQK[qslot]], writes=[bsps])
                for g in (0, 1):
                    sps, bsps = SPS[g]
                    ep, bep = EP[g][pr % 3], bEP[g][pr % 3]
                    kb.op("act", lambda s, e=ep, ps=sps[:, :]:
                          s.activation(e, ps, AF.Exp, scale=0.125),
                          reads=[bsps], writes=[bep])
                for k_ in (0, 1):
                    st = st0 + k_
                    if st >= 4 * tc:
                        w = (st - 4 * tc) * 128
                        o = k_ * 512
                        for g in (0, 1):
                            ep, bep = EP[g][pr % 3], bEP[g][pr % 3]
                            if w > 0:
                                kb.op("dve", lambda v, e=ep[:, o:o + w]: v.memset(e, 0.0),
                                      writes=[bep])
                            kb.op("dve", lambda v, e=ep[:, o + w:o + w + 128]:
                                  v.tensor_mul(e, e, MSK[:, :]),
                                  reads=[bCONST], writes=[bep])
                if pr > 0:
                    emit_avden(pr - 1, tc)
                if pr == 1:
                    flush_pending()
            emit_avden(npr - 1, tc)

            par = tc % 2
            ysl, bysl = ycomb[h]
            dst = ysl[:, tc * 512:(tc + 1) * 512]
            for g in (0, 1):
                yps, byps = YPS[g]
                kb.op("dve", lambda v, d=YR[g][par][:, :], ps=yps[:, :]:
                      v.tensor_copy(d, ps), reads=[byps], writes=[bYR[g][par]])
            for g in (0, 1):
                dps, bdps = DPS[g]
                kb.op("dve", lambda v, d=DR[g][par][:, :], ps=dps[:, :]:
                      v.tensor_copy(d, ps), reads=[bdps], writes=[bDR[g][par]])
            for g in (0, 1):
                pending.append(("act",
                                (lambda s, dr=DR[g][par][:, :]:
                                 s.activation(USC[:, :], dr, AF.Ln)),
                                [bDR[g][par]], [bUSC]))
                pending.append(("act",
                                (lambda s: s.activation(BSB[:, :], USC[:, :],
                                                        AF.Exp, scale=-1.0)),
                                [bUSC], [bBSB]))
                if g == 0:
                    pending.append(("dve",
                                    (lambda v, d=dst, yr=YR[0][par][:, :]:
                                     v.tensor_mul(d, yr, BSB[:, :])),
                                    [bYR[0][par], bBSB], [bysl]))
                else:
                    pending.append(("dve",
                                    (lambda v: v.tensor_scalar(BSB[:, :], BSB[:, :],
                                                               LSC[:, 2:3], None, ALU.mult)),
                                    [bLSC, bBSB], [bBSB]))
                    pending.append(("dve",
                                    (lambda v, yr=YR[1][par][:, :]:
                                     v.tensor_mul(USC[:, :], yr, BSB[:, :])),
                                    [bYR[1][par], bBSB], [bUSC]))
                    pending.append(("dve",
                                    (lambda v, d=dst: v.tensor_add(d, d, USC[:, :])),
                                    [bUSC, bysl], [bysl]))

    flush_pending()

    if debug_stop == "attn":
        return _finish_debug(nc, kb, QK, VSB, YEX, bQK, bV, bYEX)

    for c in range(8):
        _seed(bZX[c], *(bV[4 * c:4 * c + 4] if c < 4
                        else [bQK[8 + 2 * (c - 4)], bQK[9 + 2 * (c - 4)]]))
    _x0src = [bDR[0][0], bDR[0][1], bDR[1][0], bDR[1][1],
              bABT, bABT, bXS[0], bXS[1]]
    for i in range(8):
        _seed(bX0[i], _x0src[i])
    _seed(bX0[4], bEP[1][2])
    _seed(bX0[5], bEP[1][2])
    _seed(bX0[6], bEP[0][2])
    _seed(bMT, bYR[0][0])
    _seed(bRT, bYR[0][1])
    _seed(bMT2, bYR[1][0])
    _seed(bRT2, bYR[1][1])

    def x0_load(tc):
        for c in range(8):
            kb.dma("sp", "dSP", lambda g, c=c, tc=tc: g.dma_start(
                out=X0T[c][:, :],
                in_=d_xT[c * 128:(c + 1) * 128, tc * 512:(tc + 1) * 512]),
                writes=[bX0[c]])

    x0_load(0)
    PRJ = {0: (PS4, bPS4), 1: (PS5, bPS5)}
    for tc in range(NTC):
        for oc in range(NDC):
            prj, bprj = PRJ[oc % 2]
            for kc in range(8):
                ysl, bysl = ycomb[kc]
                lhs = WP[:, kc * 1024 + oc * 128: kc * 1024 + (oc + 1) * 128]
                rhs = ysl[:, tc * 512:(tc + 1) * 512]
                kb.op("pe", lambda t, o=prj[:, :], l=lhs, r=rhs,
                      s=(kc == 0), e=(kc == 7):
                      t.matmul(o, l, r, start=s, stop=e, tile_position=(0, 0)),
                      reads=[bWc["wp"][kc // 2], bysl], writes=[bprj])
            kb.op("dve", lambda v, d=ZX[oc][:, tc * 512:(tc + 1) * 512],
                  ps=prj[:, :], x0=X0T[oc][:, :]: v.tensor_add(d, ps, x0),
                  reads=[bprj, bX0[oc]], writes=[bZX[oc]])
        if tc + 1 < NTC:
            x0_load(tc + 1)

    if debug_stop == "proj":
        return _finish_debug(nc, kb, QK, None, YEX, bQK, bV, bYEX, ZX=ZX, bZX=bZX, nqk=8)

    for f in range(32):
        _seed(bHT[f], ycomb[{0: 2, 1: 4, 2: 6, 3: 3, 4: 5, 5: 7, 6: 0, 7: 1}[f // 4]][1])
    for c in range(8):
        _seed(bX1B[c], bQK[3] if c < 4 else bQK[7])
    for q_ in range(4):
        _seed(bPQS[q_], bEP[1][q_ // 2])
        _seed(bZBS[q_], bEP[0][q_ // 2], bQNS[q_ // 2])

    def ln_block(tc, out_x1b, ps_s, bps_s, ps_q, bps_q, mt, bmt, rt, brt,
                 scr0, bscr0, scr1, bscr1):
        zb = (ZBS[0][:, :], ZBS[1][:, :])
        sq = (ZBS[2][:, :], ZBS[3][:, :])
        for oc in range(8):
            zsl = ZX[oc][:, tc * 512:(tc + 1) * 512]
            kb.op("dve", lambda v, d=zb[oc % 2], s=zsl: v.tensor_copy(d, s),
                  reads=[bZX[oc]], writes=[bZBS[oc % 2]])
            kb.op("act", lambda s_, d=sq[oc % 2], s=zsl: s_.activation(d, s, AF.Square),
                  reads=[bZX[oc]], writes=[bZBS[2 + oc % 2]])
            kb.op("pe", lambda t, o=ps_s[:, 0:512], r=zb[oc % 2], s=(oc == 0), e=(oc == 7):
                  t.matmul(o, ONES[:, :], r, start=s, stop=e, tile_position=(0, 0)),
                  reads=[bCONST, bZBS[oc % 2]], writes=[bps_s])
            kb.op("pe", lambda t, o=ps_q[:, 0:512], r=sq[oc % 2], s=(oc == 0), e=(oc == 7):
                  t.matmul(o, ONES[:, :], r, start=s, stop=e, tile_position=(0, 0)),
                  reads=[bCONST, bZBS[2 + oc % 2]], writes=[bps_q])
        kb.op("dve", lambda v: v.tensor_scalar(mt[:, :], ps_s[:, 0:512], 1.0 / 1024.0,
                                               None, ALU.mult),
              reads=[bps_s], writes=[bmt])
        kb.op("dve", lambda v: v.tensor_mul(scr0[:, :], mt[:, :], mt[:, :]),
              reads=[bmt], writes=[bscr0])
        kb.op("dve", lambda v: v.scalar_tensor_tensor(rt[:, :], ps_q[:, 0:512], 1.0 / 1024.0,
                                                      scr0[:, :], ALU.mult, ALU.subtract),
              reads=[bps_q, bscr0], writes=[brt])
        kb.op("act", lambda s: s.activation(scr0[:, :], rt[:, :], AF.Ln, bias=LSC[:, 4:5]),
              reads=[brt, bLSC], writes=[bscr0])
        kb.op("act", lambda s: s.activation(rt[:, :], scr0[:, :], AF.Exp, scale=-0.5),
              reads=[bscr0], writes=[brt])
        for oc in range(8):
            zsl = ZX[oc][:, tc * 512:(tc + 1) * 512]
            scr, bscr = (scr0, bscr0) if oc % 2 == 0 else (scr1, bscr1)
            kb.op("dve", lambda v, z=zsl, d=scr[:, :], m=mt[:, :]: v.tensor_sub(d, z, m),
                  reads=[bZX[oc], bmt], writes=[bscr])
            kb.op("dve", lambda v, z=zsl, d=scr[:, :], r=rt[:, :]: v.tensor_mul(z, d, r),
                  reads=[bscr, brt, bZX[oc]], writes=[bZX[oc]])
            if out_x1b:
                kb.op("dve", lambda v, d=X1B[oc][:, :], z=zsl: v.tensor_copy(d, z),
                      reads=[bZX[oc]], writes=[bX1B[oc]])

    ln_block(0, True, PS6, bPS6, PS7, bPS7, MT, bMT, RT, bRT,
             USC, bUSC, BSB, bBSB)
    for tc in range(NTC):
        for rc in range(2):
            for dc in range(8):
                lhs = U1S[:, dc * 256 + rc * 128: dc * 256 + (rc + 1) * 128]
                kb.op("pe", lambda t, o=PS4[:, :], l=lhs, r=X1B[dc][:, :],
                      s=(dc == 0), e=(dc == 7):
                      t.matmul(o, l, r, start=s, stop=e, tile_position=(0, 0)),
                      reads=[bW["u1"], bX1B[dc]], writes=[bPS4])
            kb.op("act", lambda s, d=PQS[:, rc * 512:(rc + 1) * 512], ps=PS4[:, :]:
                  s.copy(d, ps), reads=[bPS4], writes=[bPQS[rc]])
        for fc in range(32):
            hps, bhps = (PS5, bPS5) if fc % 2 == 0 else (PS4, bPS4)
            for rc in range(2):
                lhs = V1S[:, rc * 4096 + fc * 128: rc * 4096 + (fc + 1) * 128]
                rhs = PQS[:, rc * 512:(rc + 1) * 512]
                kb.op("pe", lambda t, o=hps[:, :], l=lhs, r=rhs,
                      s=(rc == 0), e=(rc == 1):
                      t.matmul(o, l, r, start=s, stop=e, tile_position=(0, 0)),
                      reads=[bW["v1"], bPQS[rc]], writes=[bhps])
            kb.op("act", lambda s, d=HT[fc][:, :], ps=hps[:, :], b=B1C[:, fc:fc + 1]:
                  s.activation(d, ps, AF.Gelu_apprx_tanh, bias=b),
                  reads=[bhps, bCONST], writes=[bHT[fc]])
        for rc in range(2):
            for fc in range(32):
                lhs = U2S[:, fc * 256 + rc * 128: fc * 256 + (rc + 1) * 128]
                kb.op("pe", lambda t, o=PS7[:, :], l=lhs, r=HT[fc][:, :],
                      s=(fc == 0), e=(fc == 31):
                      t.matmul(o, l, r, start=s, stop=e, tile_position=(0, 0)),
                      reads=[bW["u2"], bHT[fc]], writes=[bPS7])
            kb.op("act", lambda s, d=PQS[:, 1024 + rc * 512: 1024 + (rc + 1) * 512],
                  ps=PS7[:, :]: s.copy(d, ps),
                  reads=[bPS7], writes=[bPQS[2 + rc]])
        if tc + 1 < NTC:
            ln_block(tc + 1, True, PS6, bPS6, PS7, bPS7, MT, bMT, RT, bRT,
                     USC, bUSC, BSB, bBSB)
        for oc in range(NDC):
            ops_, bops = (PSA, bPSA) if oc % 2 == 0 else (PSB, bPSB)
            for rc in range(2):
                lhs = V2S[:, rc * 1024 + oc * 128: rc * 1024 + (oc + 1) * 128]
                rhs = PQS[:, 1024 + rc * 512: 1024 + (rc + 1) * 512]
                kb.op("pe", lambda t, o=ops_[:, 0:512], l=lhs, r=rhs,
                      s=(rc == 0), e=(rc == 1):
                      t.matmul(o, l, r, start=s, stop=e, tile_position=(0, 0)),
                      reads=[bW["v2"], bPQS[2 + rc]], writes=[bops])
            kb.op("dve", lambda v, d=ZX[oc][:, tc * 512:(tc + 1) * 512],
                  ps=ops_[:, 0:512], b=B2C[:, oc:oc + 1]:
                  v.scalar_tensor_tensor(d, ps, b, d, ALU.add, ALU.add),
                  reads=[bops, bCONST, bZX[oc]], writes=[bZX[oc]])
        ln_block(tc, False, PS5, bPS5, PS4, bPS4, MT2, bMT2, RT2, bRT2,
                 X0T[0], bX0[0], X0T[1], bX0[1])
        for oc in range(NDC):
            kb.dma("sp", "dSP", lambda g, oc=oc, tc=tc: g.dma_start(
                out=d_out[oc * 128:(oc + 1) * 128, tc * 512:(tc + 1) * 512],
                in_=ZX[oc][:, tc * 512:(tc + 1) * 512]),
                reads=[bZX[oc]])

    return _finish(nc, kb)


def _finish(nc, kb):
    with nc.Block() as block:
        @block.sync
        def _(sync):
            kb.replay("sp", sync)
            for i in range(kb.npool["dSP"]):
                sync.wait_ge(kb.semh[f"dSP{i}"], kb.cnt[f"dSP{i}"])

        @block.tensor
        def _(tensor):
            kb.replay("pe", tensor)

        @block.scalar
        def _(scalar):
            kb.replay("act", scalar)

        @block.vector
        def _(vector):
            kb.replay("dve", vector)

        @block.gpsimd
        def _(gpsimd):
            kb.replay("pool", gpsimd)
    return nc


def _finish_debug(nc, kb, QK, VSB, YEX, bQK, bV, bYEX, ZX=None, bZX=None, nqk=16):
    d_dbg_qk = nc.dram_tensor("dbg_qk", [128, 16 * T], BF16, kind="ExternalOutput").ap()
    extra = []
    for b in bQK + bV + bYEX + (bZX or []):
        extra += list(b.w.items())
        extra += list(b.r.items())
    for i in range(nqk):
        kb.dma("sp", "dSP", lambda g, i=i: g.dma_start(
            out=d_dbg_qk[:, i * T:(i + 1) * T], in_=QK[i][:, :]), extra=extra)
    if VSB is not None:
        d_dbg_v = nc.dram_tensor("dbg_v", [128, NT * 1024], BF16, kind="ExternalOutput").ap()
        kb.dma("sp", "dSP", lambda g: g.dma_start(out=d_dbg_v, in_=VSB[:, :]), extra=extra)
    if YEX is not None:
        d_dbg_yex = nc.dram_tensor("dbg_yex", [128, 2 * T], BF16, kind="ExternalOutput").ap()
        for i in range(2):
            kb.dma("sp", "dSP", lambda g, i=i: g.dma_start(
                out=d_dbg_yex[:, i * T:(i + 1) * T], in_=YEX[i][:, :]), extra=extra)
    if ZX is not None:
        d_dbg_z = nc.dram_tensor("dbg_z", [128, 8 * T], F32, kind="ExternalOutput").ap()
        for i in range(8):
            kb.dma("sp", "dSP", lambda g, i=i: g.dma_start(
                out=d_dbg_z[:, i * T:(i + 1) * T], in_=ZX[i][:, :]), extra=extra)
    return _finish(nc, kb)


def _rotary_tables():
    import jax
    with jax.default_device(jax.devices("cpu")[0]):
        import jax.numpy as jnp
        inv_freq = 1.0 / 10000.0 ** (jnp.arange(0, 64, 2, dtype=jnp.float32) / 64)
        freqs = jnp.outer(jnp.arange(T, dtype=jnp.float32), inv_freq)
        return np.asarray(jnp.cos(freqs)), np.asarray(jnp.sin(freqs))


def stage_inputs(L0_attn_x, L0_attn_Wq, L0_attn_Wk, L0_attn_Wv, L0_attn_Wproj,
                 L0_attn_lambda_q1, L0_attn_lambda_k1, L0_attn_lambda_q2,
                 L0_attn_lambda_k2, L0_ffn_U1, L0_ffn_V1, L0_ffn_b1,
                 L0_ffn_U2, L0_ffn_V2, L0_ffn_b2):
    cos, sin = _rotary_tables()

    cos_t = cos.reshape(16, 128, 32).transpose(1, 0, 2)
    sin_t = sin.reshape(16, 128, 32).transpose(1, 0, 2)
    abt = np.concatenate([cos_t, cos_t, sin_t, -sin_t], axis=2)
    abt = np.ascontiguousarray(abt.reshape(128, 16 * 128)).astype(np.float32)

    s_i, t_i = np.arange(128)[:, None], np.arange(128)[None, :]
    lamr = np.concatenate([
        np.broadcast_to(np.asarray(v, np.float32)[None, :], (128, 32))
        for v in (L0_attn_lambda_q1, L0_attn_lambda_k1, L0_attn_lambda_q2, L0_attn_lambda_k2)
    ], axis=1)
    shared = {
        "wqT": np.ascontiguousarray(np.asarray(L0_attn_Wq, np.float32).T),
        "wkT": np.ascontiguousarray(np.asarray(L0_attn_Wk, np.float32).T),
        "wvT": np.ascontiguousarray(np.asarray(L0_attn_Wv, np.float32).T),
        "wpT": np.ascontiguousarray(np.asarray(L0_attn_Wproj, np.float32).T),
        "u1": np.ascontiguousarray(np.asarray(L0_ffn_U1, np.float32)),
        "v1f": np.ascontiguousarray(np.asarray(L0_ffn_V1, np.float32)),
        "u2": np.ascontiguousarray(np.asarray(L0_ffn_U2, np.float32)),
        "v2f": np.ascontiguousarray(np.asarray(L0_ffn_V2, np.float32)),
        "b1c": np.ascontiguousarray(np.asarray(L0_ffn_b1, np.float32).reshape(32, 128).T),
        "b2c": np.ascontiguousarray(np.asarray(L0_ffn_b2, np.float32).reshape(8, 128).T),
        "lamr": np.ascontiguousarray(lamr.astype(np.float32)),
        "abtT": abt,
        "maskb": np.ascontiguousarray((t_i >= s_i).astype(ml_dtypes.bfloat16)),
        "identb": np.ascontiguousarray(np.eye(128, dtype=ml_dtypes.bfloat16)),
    }
    x = np.asarray(L0_attn_x, np.float32)
    maps = []
    for b in range(8):
        xb = x[b]
        x1p = xb.reshape(16, 128, 8, 128).transpose(3, 0, 2, 1).reshape(128, -1)
        maps.append(dict(shared, xT=np.ascontiguousarray(xb.T),
                         x1p=np.ascontiguousarray(x1p)))
    return maps


def kernel(L0_attn_x, L0_attn_Wq, L0_attn_Wk, L0_attn_Wv, L0_attn_Wproj,
           L0_attn_lambda_q1, L0_attn_lambda_k1, L0_attn_lambda_q2,
           L0_attn_lambda_k2, L0_ffn_U1, L0_ffn_V1, L0_ffn_b1,
           L0_ffn_U2, L0_ffn_V2, L0_ffn_b2):
    in_maps = stage_inputs(L0_attn_x, L0_attn_Wq, L0_attn_Wk, L0_attn_Wv,
                           L0_attn_Wproj, L0_attn_lambda_q1, L0_attn_lambda_k1,
                           L0_attn_lambda_q2, L0_attn_lambda_k2, L0_ffn_U1,
                           L0_ffn_V1, L0_ffn_b1, L0_ffn_U2, L0_ffn_V2, L0_ffn_b2)
    nc = build_core()
    res = run_bass_kernel_spmd(nc, in_maps, core_ids=list(range(8)))
    out = np.stack([np.ascontiguousarray(r["outT"].T) for r in res.results])
    return out.astype(np.float32)


# version 34
# speedup vs baseline: 1.0120x; 1.0120x over previous
import math
import os

import numpy as np
import ml_dtypes
import concourse.bass as bass
import concourse.mybir as mybir
from concourse.bass_utils import run_bass_kernel_spmd

F32 = mybir.dt.float32
BF16 = mybir.dt.bfloat16
AF = mybir.ActivationFunctionType
ALU = mybir.AluOpType
AXX = mybir.AxisListType.X

T = 2048
D = 1024
NT = 16
NDC = 8
NTC = 4
DFF = 4096
RANK = 256

EPS_RMS = 2.220446049250313e-16 if os.environ.get("BASS_SIM_FP64") == "1" else 1.1920929e-07
EPS_LN = 1e-5
LAMBDA_INIT = 0.8 - 0.6 * math.exp(-0.3 * 1)


class _Buf:
    __slots__ = ("w", "r")

    def __init__(self):
        self.w = {}
        self.r = {}


def _seed(dst, *srcs):
    for s in srcs:
        for k, v in s.w.items():
            dst.r[k] = max(dst.r.get(k, 0), v)
        for k, v in s.r.items():
            dst.r[k] = max(dst.r.get(k, 0), v)


class _KB:

    def __init__(self, nc):
        self.nc = nc
        self.steps = {e: [] for e in ("pe", "act", "dve", "pool", "sp")}
        self.semh = {}
        self.cnt = {}
        names = ["cPE", "cACT", "cDVE", "cPOOL"]
        names += [f"dGP{i}" for i in range(10)] + [f"dSP{i}" for i in range(4)]
        for s in names:
            self.semh[s] = nc.alloc_semaphore(s)
            self.cnt[s] = 0
        self.clk = {"pe": "cPE", "act": "cACT", "dve": "cDVE", "pool": "cPOOL"}
        self.rr = {"dGP": 0, "dSP": 0}
        self.npool = {"dGP": 10, "dSP": 4}

    def _emit(self, eng, fn, amount, sem, reads, writes, extra):
        waits = list(extra)
        for b in reads:
            waits += b.w.items()
        for b in writes:
            waits += b.w.items()
            waits += b.r.items()
        own = self.clk.get(eng) if eng == "pe" else None
        w = {}
        for s, v in waits:
            if s is None or s == own:
                continue
            w[s] = max(w.get(s, 0), v)
        self.cnt[sem] += amount
        ts = (sem, self.cnt[sem])
        self.steps[eng].append((w, fn, ts, amount))
        for b in reads:
            b.r[ts[0]] = max(b.r.get(ts[0], 0), ts[1])
        for b in writes:
            b.w = {ts[0]: ts[1]}
            b.r = {}
        return ts

    def op(self, eng, fn, reads=(), writes=(), extra=()):
        return self._emit(eng, fn, 1, self.clk[eng], reads, writes, extra)

    def dma(self, eng, queue, fn, reads=(), writes=(), extra=()):
        sem = f"{queue}{self.rr[queue]}"
        self.rr[queue] = (self.rr[queue] + 1) % self.npool[queue]
        extra = list(extra) + [(sem, self.cnt[sem])]
        return self._emit(eng, fn, 16, sem, reads, writes, extra)

    def freeze(self):
        awaited = {}
        for steps in self.steps.values():
            for w, fn, ts, amount in steps:
                for s, v in w.items():
                    if v > 0:
                        awaited.setdefault(s, set()).add(v)
        bysem = {}
        for steps in self.steps.values():
            for w, fn, ts, amount in steps:
                bysem.setdefault(ts[0], []).append((ts[1], amount))
        self.remap = {}
        self.newtot = dict(self.cnt)
        for s, lst in bysem.items():
            lst.sort()
            m = {}
            run = 0
            aw = awaited.get(s, ())
            for oldc, amount in lst:
                keep = amount == 16 or oldc in aw
                if keep:
                    run += amount
                    m[oldc] = run
                else:
                    m[oldc] = None
            self.remap[s] = m
            self.newtot[s] = run

    def replay(self, eng_key, eng_obj):
        hw = {}
        for w, fn, ts, amount in self.steps[eng_key]:
            for s, v in sorted(w.items()):
                if v == 0:
                    continue
                nv = self.remap[s][v]
                assert nv is not None
                if hw.get(s, 0) < nv:
                    eng_obj.wait_ge(self.semh[s], nv)
                    hw[s] = nv
            ins = fn(eng_obj)
            if self.remap[ts[0]][ts[1]] is not None:
                ins.then_inc(self.semh[ts[0]], amount)


def build_core(debug_stop=None):
    nc = bass.Bass("TRN2", target_bir_lowering=False)

    d_xT = nc.dram_tensor("xT", [D, T], F32, kind="ExternalInput").ap()
    d_x1 = nc.dram_tensor("x1p", [128, NT * 8 * 128], F32, kind="ExternalInput").ap()
    d_wq = nc.dram_tensor("wqT", [D, D], F32, kind="ExternalInput").ap()
    d_wk = nc.dram_tensor("wkT", [D, D], F32, kind="ExternalInput").ap()
    d_wv = nc.dram_tensor("wvT", [D, D], F32, kind="ExternalInput").ap()
    d_wp = nc.dram_tensor("wpT", [D, D], F32, kind="ExternalInput").ap()
    d_u1 = nc.dram_tensor("u1", [D, RANK], F32, kind="ExternalInput").ap()
    d_v1 = nc.dram_tensor("v1f", [RANK, DFF], F32, kind="ExternalInput").ap()
    d_u2 = nc.dram_tensor("u2", [DFF, RANK], F32, kind="ExternalInput").ap()
    d_v2 = nc.dram_tensor("v2f", [RANK, D], F32, kind="ExternalInput").ap()
    d_b1 = nc.dram_tensor("b1c", [128, 32], F32, kind="ExternalInput").ap()
    d_b2 = nc.dram_tensor("b2c", [128, 8], F32, kind="ExternalInput").ap()
    d_lam = nc.dram_tensor("lamr", [128, 128], F32, kind="ExternalInput").ap()
    d_abt = nc.dram_tensor("abtT", [128, NT * 128], F32, kind="ExternalInput").ap()
    d_mask = nc.dram_tensor("maskb", [128, 128], BF16, kind="ExternalInput").ap()
    d_ident = nc.dram_tensor("identb", [128, 128], BF16, kind="ExternalInput").ap()
    d_out = nc.dram_tensor("outT", [D, T], F32, kind="ExternalOutput").ap()

    off = [16512]
    addr = {}

    def place(name, shape, dtype, at=None):
        nbytes = int(np.prod(shape[1:])) * (2 if dtype == BF16 else 4)
        o = at if at is not None else off[0]
        h = nc.alloc_sbuf_tensor_at(name, shape, dtype, offset=o)
        addr[name] = o
        if at is None:
            off[0] += (nbytes + 63) & ~63
        return h

    XSTG = [place(f"xstg{i}", [128, 8 * 128], BF16) for i in range(4)]
    QK = [place(f"qk{i}", [128, T], BF16) for i in range(16)]
    VSB = place("vsb", [128, NT * 1024], BF16)
    YEX = [place(f"yex{i}", [128, T], BF16) for i in range(2)]
    QF = place("qf", [128, 1024], F32)
    SQ = place("sq", [128, 1024], F32)
    ROT = place("rot", [128, 2048], BF16)
    WQ = place("wq_sb", [128, 8 * 1024], BF16)
    WK = place("wk_sb", [128, 8 * 1024], BF16)
    WV = place("wv_sb", [128, 8 * 1024], BF16)
    WP = place("wp_sb", [128, 8 * 1024], BF16)
    ESA = place("esa", [128, 2048], BF16)
    BSB = place("bsb", [128, 512], F32)
    PQS = place("pqs", [128, 2048], BF16)
    USC = place("usc", [128, 512], F32)
    TTS = place("tts_sb", [128, 16 * 96], F32)
    MSK = place("mask_sb", [128, 128], BF16)
    IDT = place("ident_sb", [128, 128], BF16)
    ONES = place("ones_sb", [128, 128], BF16)
    B1C = place("b1c_sb", [128, 32], F32)
    B2C = place("b2c_sb", [128, 8], F32)
    LAMR = place("lamr_sb", [128, 128], F32)
    LSC = place("lsc", [128, 8], F32)
    MS = place("ms", [128, 16], F32)
    RS = place("rs", [128, 16], F32)
    MS2 = place("ms2", [128, 16], F32)
    RS2 = place("rs2", [128, 16], F32)
    assert off[0] <= 229344, off[0]
    MT = place("mt", [128, 512], F32, at=addr["tts_sb"])
    RT = place("rt", [128, 512], F32, at=addr["tts_sb"] + 2048)
    MT2 = place("mt2", [128, 512], F32, at=addr["tts_sb"] + 4096)
    RT2 = place("rt2", [128, 512], F32, at=addr["xstg2"])
    YR = [[place("yr00", [128, 512], F32, at=addr["tts_sb"]),
           place("yr01", [128, 512], F32, at=addr["tts_sb"] + 2048)],
          [place("yr10", [128, 512], F32, at=addr["tts_sb"] + 4096),
           place("yr11", [128, 512], F32, at=addr["xstg2"])]]
    DR = [[place("dr00", [128, 512], F32, at=addr["qf"]),
           place("dr01", [128, 512], F32, at=addr["qf"] + 2048)],
          [place("dr10", [128, 512], F32, at=addr["qf"] + 4096),
           place("dr11", [128, 512], F32, at=addr["qf"] + 6144)]]

    QB = place("qbn", [128, 1024], BF16, at=addr["yex0"])
    KB_ = place("kbn", [128, 1024], BF16, at=addr["yex0"] + 2048)
    SQK = place("sqk", [128, 1024], F32, at=addr["yex1"])
    QNS = [place("qns0", [128, 1024], BF16, at=addr["esa"]),
           place("qns1", [128, 1024], BF16, at=addr["esa"] + 2048)]
    WBS = [place("wbs0", [128, 1024], BF16, at=addr["pqs"]),
           place("wbs1", [128, 1024], BF16, at=addr["pqs"] + 2048)]
    ABT = place("abt_sb", [128, NT * 128], BF16, at=addr["rot"])
    RSE = [place("rse0", [128, 1024], BF16, at=addr["sq"]),
           place("rse1", [128, 1024], BF16, at=addr["sqk"])]
    X0T = []
    for i in range(8):
        at = addr["qf"] + 2048 * i if i < 6 else addr["xstg0"] + 2048 * (i - 6)
        X0T.append(place(f"x0t{i}", [128, 512], F32, at=at))
    X1B = [place(f"x1b{c}", [128, 512], BF16,
                 at=addr["qk3" if c < 4 else "qk7"] + 1024 * (c % 4)) for c in range(8)]
    _h_homes = ["qk0", "qk1", "qk2", "qk4", "qk5", "qk6", "yex0", "yex1"]
    HT = [place(f"ht{f}", [128, 512], BF16,
                at=addr[_h_homes[f // 4]] + 1024 * (f % 4)) for f in range(32)]
    ZX = [place(f"zx{c}", [128, T], F32,
                at=(addr["vsb"] + 8192 * c) if c < 4 else (addr[f"qk{8 + 2 * (c - 4)}"]))
          for c in range(8)]
    ZBS = [place(f"zbs{i}", [128, 512], BF16, at=addr["esa"] + 1024 * i)
           for i in range(4)]
    U1S = place("u1_sb", [128, 8 * 256], BF16, at=addr["wq_sb"])
    V2S = place("v2_sb", [128, 2 * 1024], BF16, at=addr["wq_sb"] + 4096)
    V1S = place("v1_sb", [128, 2 * 4096], BF16, at=addr["wk_sb"])
    U2S = place("u2_sb", [128, 32 * 256], BF16, at=addr["wv_sb"])

    PSA = nc.alloc_psum_tensor("psA", [128, 1024], F32)
    PSB = nc.alloc_psum_tensor("psB", [128, 1024], F32)
    PS4 = nc.alloc_psum_tensor("ps4", [128, 512], F32)
    PS5 = nc.alloc_psum_tensor("ps5", [128, 512], F32)
    PS6 = nc.alloc_psum_tensor("ps6", [128, 512], F32)
    PS7 = nc.alloc_psum_tensor("ps7", [128, 512], F32)

    kb = _KB(nc)

    bXS = [_Buf() for _ in range(4)]
    bQK = [_Buf() for _ in range(16)]
    bV = [_Buf() for _ in range(16)]
    bYEX = [_Buf() for _ in range(2)]
    bQF, bSQ, bQB, bKB, bSQK = _Buf(), _Buf(), _Buf(), _Buf(), _Buf()
    bQNS = [_Buf(), _Buf()]
    bWBS = [_Buf(), _Buf()]
    bABT = _Buf()
    bW = {k: _Buf() for k in ("u1", "v1", "u2", "v2")}
    bWc = {k: [_Buf() for _ in range(4)] for k in ("wq", "wk", "wv", "wp")}
    bEP = [[_Buf(), _Buf()], [_Buf(), _Buf()]]
    bBSB, bUSC = _Buf(), _Buf()
    bPQS = [_Buf() for _ in range(4)]
    bCONST = _Buf()
    bLSC = _Buf()
    bMS, bRS, bMS2, bRS2, bMT, bRT, bMT2, bRT2 = (_Buf() for _ in range(8))
    bYR = [[_Buf(), _Buf()], [_Buf(), _Buf()]]
    bDR = [[_Buf(), _Buf()], [_Buf(), _Buf()]]
    bPSA, bPSB, bPS4, bPS5, bPS6, bPS7 = (_Buf() for _ in range(6))
    bX0 = [_Buf() for _ in range(8)]
    bHT = [_Buf() for _ in range(32)]
    bZX = [_Buf() for _ in range(8)]
    bX1B = [_Buf() for _ in range(8)]
    bZBS = [_Buf() for _ in range(4)]

    def ld(dst_ap, src_ap):
        return lambda g: g.dma_start(out=dst_ap, in_=src_ap)

    def wload_chunk(dst, src, n, ch, bufs, extra=()):
        dv = dst[:, :].rearrange("p (c n) -> p c n", n=n)[:, 2 * ch:2 * ch + 2, :]
        sv = src.rearrange("(c p) n -> p c n", p=128)[:, 2 * ch:2 * ch + 2, :]
        kb.dma("pool", "dGP", lambda g, d=dv, s=sv: g.dma_start(out=d, in_=s),
               writes=[bufs[ch]], extra=extra)

    for ch in range(4):
        wload_chunk(WQ, d_wq, 1024, ch, bWc["wq"])
    kb.dma("pool", "dGP", ld(ABT[:, :], d_abt), writes=[bABT])
    kb.dma("sp", "dSP", ld(MSK[:, :], d_mask), writes=[bCONST])
    kb.dma("sp", "dSP", ld(IDT[:, :], d_ident), writes=[bCONST])
    kb.dma("sp", "dSP", ld(B1C[:, :], d_b1), writes=[bCONST])
    kb.dma("sp", "dSP", ld(B2C[:, :], d_b2), writes=[bCONST])
    ts_lam = kb.dma("sp", "dSP", ld(LAMR[:, :], d_lam), writes=[bCONST])

    wpend = [(WK, d_wk, ch, bWc["wk"]) for ch in range(4)]
    wpend += [(WV, d_wv, ch, bWc["wv"]) for ch in range(4)]
    wpend += [(WP, d_wp, ch, bWc["wp"]) for ch in range(4)]

    kb.op("dve", lambda v: v.memset(ONES[:, :], 1.0), writes=[bCONST])
    kb.op("dve", lambda v: v.memset(LSC[:, 3:4], float(EPS_RMS)), writes=[bLSC])
    kb.op("dve", lambda v: v.memset(LSC[:, 4:5], float(EPS_LN)), writes=[bLSC])

    kb.op("dve", lambda v: v.tensor_mul(LAMR[:, 0:32], LAMR[:, 0:32], LAMR[:, 32:64]),
          reads=[bCONST], writes=[bCONST], extra=[ts_lam])
    kb.op("dve", lambda v: v.tensor_mul(LAMR[:, 64:96], LAMR[:, 64:96], LAMR[:, 96:128]),
          reads=[bCONST], writes=[bCONST])
    kb.op("dve", lambda v: v.reduce_sum(LSC[:, 0:1], LAMR[:, 0:32], axis=AXX),
          reads=[bCONST], writes=[bLSC])
    kb.op("dve", lambda v: v.reduce_sum(LSC[:, 1:2], LAMR[:, 64:96], axis=AXX),
          reads=[bCONST], writes=[bLSC])
    kb.op("act", lambda s: s.activation(LSC[:, 0:1], LSC[:, 0:1], AF.Exp),
          reads=[bLSC], writes=[bLSC])
    kb.op("act", lambda s: s.activation(LSC[:, 1:2], LSC[:, 1:2], AF.Exp),
          reads=[bLSC], writes=[bLSC])
    kb.op("dve", lambda v: v.tensor_sub(LSC[:, 2:3], LSC[:, 0:1], LSC[:, 1:2]),
          reads=[bLSC], writes=[bLSC])
    kb.op("dve", lambda v: v.tensor_scalar(LSC[:, 2:3], LSC[:, 2:3],
                                           float(LAMBDA_INIT), -1.0, ALU.add, ALU.mult),
          reads=[bLSC], writes=[bLSC])

    def x_load(vt):
        i = vt % NT
        sl = vt % 4
        src = d_x1[:, i * 1024:(i + 1) * 1024]
        dst = XSTG[sl][:, :]
        kb.dma("pool", "dGP", lambda g, s=src, d=dst: g.dma_start(out=d, in_=s),
               writes=[bXS[sl]])
        if vt % 4 == 1 and wpend:
            dst_w, src_w, ch, bufs = wpend.pop(0)
            wload_chunk(dst_w, src_w, 1024, ch, bufs)

    for vt in range(3):
        x_load(vt)

    def qrf(i):
        return QK[8 + i // 2][:, (i % 2) * 1024:(i % 2 + 1) * 1024]

    def krf(i):
        return VSB[:, i * 1024:(i + 1) * 1024]

    tcount = [0]

    def emit_T(i, rfv, brf, slot0):
        _rot = ((PS6, bPS6), (PS7, bPS7), (PS4, bPS4), (PS5, bPS5))
        for fj in range(8):
            n = tcount[0]
            tcount[0] += 1
            ps, bps = _rot[n % 4]
            blk = rfv[:, fj * 128:(fj + 1) * 128]
            kb.op("pe", lambda t, o=ps[:, 0:128], l=blk:
                  t.matmul(o, l, IDT[:, :], start=True, stop=True, tile_position=(0, 0)),
                  reads=[brf, bCONST], writes=[bps])
            dst = QK[slot0 + fj][:, i * 128:(i + 1) * 128]
            if n % 2 == 0:
                kb.op("dve", lambda v, d=dst, s=ps[:, 0:128]: v.tensor_copy(d, s),
                      reads=[bps], writes=[bQK[slot0 + fj]])
            else:
                kb.op("act", lambda s_, d=dst, s=ps[:, 0:128]: s_.copy(d, s),
                      reads=[bps], writes=[bQK[slot0 + fj]])

    PSQv = ((PSA, bPSA), (PSB, bPSB))
    QBv, bQBv = (QB, KB_), (bQB, bKB)
    SQv, bSQv = (SQ, SQK), (bSQ, bSQK)
    MSv, bMSv = (MS, MS2), (bMS, bMS2)

    def emit_mms(vt, wsb, bwks):
        i, sl, par = vt % NT, vt % 4, vt % 2
        psq, bpsq = PSQv[par]
        for dc in range(NDC):
            lhs = XSTG[sl][:, dc * 128:(dc + 1) * 128]
            for half in (0, 1):
                rhs = wsb[:, dc * 1024 + half * 512: dc * 1024 + (half + 1) * 512]
                kb.op("pe", lambda t, o=psq[:, half * 512:(half + 1) * 512], l=lhs,
                      r=rhs, s=(dc == 0), e=(dc == NDC - 1):
                      t.matmul(o, l, r, start=s, stop=e, tile_position=(0, 0)),
                      reads=[bXS[sl], bwks[dc // 2]], writes=[bpsq])
        return psq, bpsq

    def emit_qk_pass(pidx, wsb, bwks, rf_of, brf_of, tp_of):
        for i in range(NT):
            par = i % 2
            psq, bpsq = emit_mms(pidx * NT + i, wsb, bwks)
            tpb = tp_of(i)
            if tpb is not None:
                emit_T(*tpb)
            qb, bqb = QBv[par], bQBv[par]
            sq, bsq = SQv[par], bSQv[par]
            ms, bms = MSv[par], bMSv[par]
            qns, bqns = QNS[par], bQNS[par]
            wb, bwb = WBS[par], bWBS[par]
            rse, brse = RSE[par], bSQv[par]
            rfv, brf = rf_of(i), brf_of(i)
            kb.op("act", lambda s, d=qb[:, :], p=psq[:, :]: s.copy(d, p),
                  reads=[bpsq], writes=[bqb])
            kb.op("act", lambda s, d=sq[:, :], p=psq[:, :]:
                  s.activation(d, p, AF.Square), reads=[bpsq], writes=[bsq])
            qv = qb[:, :].rearrange("p (h s c) -> p h s c", s=2, c=32)
            sv = qns[:, :].rearrange("p (h s c) -> p h s c", s=2, c=32)
            kb.op("pool", lambda g, d=sv[:, :, 0, :], s=qv[:, :, 1, :]:
                  g.tensor_copy(d, s), reads=[bqb], writes=[bqns])
            kb.op("pool", lambda g, d=sv[:, :, 1, :], s=qv[:, :, 0, :]:
                  g.tensor_copy(d, s), reads=[bqb], writes=[bqns])
            kb.op("dve", lambda v, o=ms[:, :], s=sq[:, :]:
                  v.reduce_sum(o, s.rearrange("p (s c) -> p s c", c=64), axis=AXX),
                  reads=[bsq], writes=[bms])
            kb.op("act", lambda s, m=ms[:, :]: s.activation(
                m, m, AF.Ln, bias=LSC[:, 3:4], scale=1.0 / 64.0),
                reads=[bms, bLSC], writes=[bms])
            rsev = rse[:, :].rearrange("p (s c) -> p s c", c=64)
            msb = ms[:, :].unsqueeze(2).broadcast_to([128, 16, 64])
            kb.op("act", lambda s, r=rsev, m=msb: s.activation(r, m, AF.Exp, scale=-0.5),
                  reads=[bms, bsq], writes=[brse])
            qn3 = qb[:, :].rearrange("p (s c) -> p s c", c=64)
            qs3 = qns[:, :].rearrange("p (s c) -> p s c", c=64)
            wb3 = wb[:, :].rearrange("p (s c) -> p s c", c=64)
            rf3 = rfv.rearrange("p (s c) -> p s c", c=64)
            ca = ABT[:, i * 128: i * 128 + 64].unsqueeze(1).broadcast_to([128, 16, 64])
            sa = ABT[:, i * 128 + 64: i * 128 + 128].unsqueeze(1).broadcast_to([128, 16, 64])
            kb.op("dve", lambda v, o=rf3, q=qn3, m=ca: v.tensor_mul(o, q, m),
                  reads=[bqb, bABT], writes=[brf])
            kb.op("dve", lambda v, o=wb3, q=qs3, m=sa: v.tensor_mul(o, q, m),
                  reads=[bqns, bABT], writes=[bwb])
            kb.op("dve", lambda v, o=qns[:, :], a=rfv, b=wb[:, :]: v.tensor_add(o, a, b),
                  reads=[bwb, brf, bqns], writes=[bqns])
            kb.op("dve", lambda v, o=rfv, a=qns[:, :], r=rse[:, :]: v.tensor_mul(o, a, r),
                  reads=[brse, bqns, brf], writes=[brf])
            if pidx * NT + i + 3 < 48:
                x_load(pidx * NT + i + 3)

    emit_qk_pass(0, WQ, bWc["wq"], qrf, lambda i: bQK[8 + i // 2], lambda i: None)
    emit_qk_pass(1, WK, bWc["wk"], krf, lambda i: bV[i],
                 lambda i: (i, qrf(i), bQK[8 + i // 2], 0))
    for i in range(NT):
        emit_T(i, krf(i), bV[i], 8)
        psq, bpsq = emit_mms(2 * NT + i, WV, bWc["wv"])
        kb.op("act", lambda s, d=VSB[:, i * 1024:(i + 1) * 1024], p=psq[:, :]:
              s.copy(d, p), reads=[bpsq], writes=[bV[i]])
        if 2 * NT + i + 3 < 48:
            x_load(2 * NT + i + 3)

    if debug_stop == "qkgen":
        return _finish_debug(nc, kb, QK, VSB, None, bQK, bV, bYEX)

    def warena(*keys):
        ex = []
        for k in keys:
            for b in bWc[k]:
                ex += list(b.w.items()) + list(b.r.items())
        return ex

    def wload(dst, src, n):
        return lambda g: g.dma_start(
            out=dst[:, :].rearrange("p (c n) -> p c n", n=n),
            in_=src.rearrange("(c p) n -> p c n", p=128))

    kb.dma("pool", "dGP", wload(V1S, d_v1, 4096), writes=[bW["v1"]],
           extra=warena("wk"))
    kb.dma("pool", "dGP", wload(U2S, d_u2, 256), writes=[bW["u2"]],
           extra=warena("wv"))
    kb.dma("pool", "dGP", wload(U1S, d_u1, 256), writes=[bW["u1"]],
           extra=warena("wq"))
    kb.dma("pool", "dGP", wload(V2S, d_v2, 1024), writes=[bW["v2"]],
           extra=warena("wq"))

    ycomb = {0: (YEX[0], bYEX[0]), 1: (YEX[1], bYEX[1]),
             2: (QK[0], bQK[0]), 3: (QK[4], bQK[4]),
             4: (QK[1], bQK[1]), 5: (QK[5], bQK[5]),
             6: (QK[2], bQK[2]), 7: (QK[6], bQK[6])}
    EA2 = place("ea2", [128, 1024], BF16, at=addr["xstg0"])
    EB2 = place("eb2", [128, 1024], BF16, at=addr["rot"])
    EP = {0: (ESA[:, 0:1024], ESA[:, 1024:2048], EA2[:, :]),
          1: (PQS[:, 0:1024], PQS[:, 1024:2048], EB2[:, :])}
    for j in range(2):
        _seed(bEP[0][j], bQNS[j])
        _seed(bEP[1][j], bWBS[j])
    bEP[0].append(_Buf())
    bEP[1].append(_Buf())
    _seed(bEP[0][2], bXS[0], bXS[1])
    _seed(bEP[1][2], bABT)
    _seed(bYEX[0], bQB)
    _seed(bYEX[1], bKB)
    _seed(bYEX[0], bKB)
    _seed(bYEX[1], bSQK)

    _seed(bYR[0][0], bCONST)
    _seed(bYR[0][1], bCONST)
    _seed(bYR[1][0], bXS[2])
    _seed(bYR[1][1], bXS[3])
    _seed(bDR[0][0], bQF)
    _seed(bDR[0][1], bQF)
    _seed(bDR[1][0], bSQ)
    _seed(bDR[1][1], bSQ)

    SPS = {0: (PSA, bPSA), 1: (PSB, bPSB)}
    YPS = {0: (PS4, bPS4), 1: (PS5, bPS5)}
    DPS = {0: (PS6, bPS6), 1: (PS7, bPS7)}

    pending = []

    def flush_pending():
        for eng, fn, reads, writes in pending:
            kb.op(eng, fn, reads=reads, writes=writes)
        pending.clear()

    for h in range(8):
        p = h // 2
        pbase = 64 * (h % 2)

        def emit_avden(pr, tc, h=h):
            st_last = 4 * (tc + 1) - 1
            for g in (0, 1):
                yps, byps = YPS[g]
                dps, bdps = DPS[g]
                ep, bep = EP[g][pr % 3], bEP[g][pr % 3]
                for k_ in (0, 1):
                    st = 2 * pr + k_
                    e = ep[:, k_ * 512:(k_ + 1) * 512]
                    vsl = VSB[:, st * 1024 + h * 128: st * 1024 + (h + 1) * 128]
                    kb.op("pe", lambda t, o=yps[:, :], l=vsl, e_=e,
                          s=(st == 0), en=(st == st_last):
                          t.matmul(o, l, e_, start=s, stop=en, tile_position=(0, 0)),
                          reads=[bV[st], bep], writes=[byps])
                    kb.op("pe", lambda t, o=dps[:, :], e_=e,
                          s=(st == 0), en=(st == st_last):
                          t.matmul(o, ONES[:, :], e_, start=s, stop=en,
                                   tile_position=(0, 0)),
                          reads=[bCONST, bep], writes=[bdps])

        for tc in range(NTC):
            npr = 2 * (tc + 1)
            for pr in range(npr):
                st0 = 2 * pr
                for g in (0, 1):
                    sps, bsps = SPS[g]
                    kslot, qslot = 8 + g * 4 + p, g * 4 + p
                    for k_ in (0, 1):
                        st = st0 + k_
                        lhs = QK[kslot][pbase:pbase + 64, st * 128:(st + 1) * 128]
                        rhs = QK[qslot][pbase:pbase + 64, tc * 512:(tc + 1) * 512]
                        kb.op("pe", lambda t, o=sps[:, k_ * 512:(k_ + 1) * 512],
                              l=lhs, r=rhs, tp=(pbase, 0):
                              t.matmul(o, l, r, start=True, stop=True, tile_position=tp),
                              reads=[bQK[kslot], bQK[qslot]], writes=[bsps])
                for g in (0, 1):
                    sps, bsps = SPS[g]
                    ep, bep = EP[g][pr % 3], bEP[g][pr % 3]
                    kb.op("act", lambda s, e=ep, ps=sps[:, :]:
                          s.activation(e, ps, AF.Exp, scale=0.125),
                          reads=[bsps], writes=[bep])
                for k_ in (0, 1):
                    st = st0 + k_
                    if st >= 4 * tc:
                        w = (st - 4 * tc) * 128
                        o = k_ * 512
                        for g in (0, 1):
                            ep, bep = EP[g][pr % 3], bEP[g][pr % 3]
                            if w > 0:
                                kb.op("dve", lambda v, e=ep[:, o:o + w]: v.memset(e, 0.0),
                                      writes=[bep])
                            kb.op("dve", lambda v, e=ep[:, o + w:o + w + 128]:
                                  v.tensor_mul(e, e, MSK[:, :]),
                                  reads=[bCONST], writes=[bep])
                if pr > 0:
                    emit_avden(pr - 1, tc)
                if pr == 1:
                    flush_pending()
            emit_avden(npr - 1, tc)

            par = tc % 2
            ysl, bysl = ycomb[h]
            dst = ysl[:, tc * 512:(tc + 1) * 512]
            for g in (0, 1):
                yps, byps = YPS[g]
                kb.op("dve", lambda v, d=YR[g][par][:, :], ps=yps[:, :]:
                      v.tensor_copy(d, ps), reads=[byps], writes=[bYR[g][par]])
            for g in (0, 1):
                dps, bdps = DPS[g]
                kb.op("dve", lambda v, d=DR[g][par][:, :], ps=dps[:, :]:
                      v.tensor_copy(d, ps), reads=[bdps], writes=[bDR[g][par]])
            for g in (0, 1):
                pending.append(("act",
                                (lambda s, dr=DR[g][par][:, :]:
                                 s.activation(USC[:, :], dr, AF.Ln)),
                                [bDR[g][par]], [bUSC]))
                pending.append(("act",
                                (lambda s: s.activation(BSB[:, :], USC[:, :],
                                                        AF.Exp, scale=-1.0)),
                                [bUSC], [bBSB]))
                if g == 0:
                    pending.append(("dve",
                                    (lambda v, d=dst, yr=YR[0][par][:, :]:
                                     v.tensor_mul(d, yr, BSB[:, :])),
                                    [bYR[0][par], bBSB], [bysl]))
                else:
                    pending.append(("dve",
                                    (lambda v: v.tensor_scalar(BSB[:, :], BSB[:, :],
                                                               LSC[:, 2:3], None, ALU.mult)),
                                    [bLSC, bBSB], [bBSB]))
                    pending.append(("dve",
                                    (lambda v, yr=YR[1][par][:, :]:
                                     v.tensor_mul(USC[:, :], yr, BSB[:, :])),
                                    [bYR[1][par], bBSB], [bUSC]))
                    pending.append(("dve",
                                    (lambda v, d=dst: v.tensor_add(d, d, USC[:, :])),
                                    [bUSC, bysl], [bysl]))

    flush_pending()

    if debug_stop == "attn":
        return _finish_debug(nc, kb, QK, VSB, YEX, bQK, bV, bYEX)

    for c in range(8):
        _seed(bZX[c], *(bV[4 * c:4 * c + 4] if c < 4
                        else [bQK[8 + 2 * (c - 4)], bQK[9 + 2 * (c - 4)]]))
    _x0src = [bDR[0][0], bDR[0][1], bDR[1][0], bDR[1][1],
              bABT, bABT, bXS[0], bXS[1]]
    for i in range(8):
        _seed(bX0[i], _x0src[i])
    _seed(bX0[4], bEP[1][2])
    _seed(bX0[5], bEP[1][2])
    _seed(bX0[6], bEP[0][2])
    _seed(bMT, bYR[0][0])
    _seed(bRT, bYR[0][1])
    _seed(bMT2, bYR[1][0])
    _seed(bRT2, bYR[1][1])

    def x0_load(tc):
        for c in range(8):
            kb.dma("sp", "dSP", lambda g, c=c, tc=tc: g.dma_start(
                out=X0T[c][:, :],
                in_=d_xT[c * 128:(c + 1) * 128, tc * 512:(tc + 1) * 512]),
                writes=[bX0[c]])

    x0_load(0)
    PRJ = {0: (PS4, bPS4), 1: (PS5, bPS5)}
    for tc in range(NTC):
        for oc in range(NDC):
            prj, bprj = PRJ[oc % 2]
            for kc in range(8):
                ysl, bysl = ycomb[kc]
                lhs = WP[:, kc * 1024 + oc * 128: kc * 1024 + (oc + 1) * 128]
                rhs = ysl[:, tc * 512:(tc + 1) * 512]
                kb.op("pe", lambda t, o=prj[:, :], l=lhs, r=rhs,
                      s=(kc == 0), e=(kc == 7):
                      t.matmul(o, l, r, start=s, stop=e, tile_position=(0, 0)),
                      reads=[bWc["wp"][kc // 2], bysl], writes=[bprj])
            kb.op("dve", lambda v, d=ZX[oc][:, tc * 512:(tc + 1) * 512],
                  ps=prj[:, :], x0=X0T[oc][:, :]: v.tensor_add(d, ps, x0),
                  reads=[bprj, bX0[oc]], writes=[bZX[oc]])
        if tc + 1 < NTC:
            x0_load(tc + 1)

    if debug_stop == "proj":
        return _finish_debug(nc, kb, QK, None, YEX, bQK, bV, bYEX, ZX=ZX, bZX=bZX, nqk=8)

    for f in range(32):
        _seed(bHT[f], ycomb[{0: 2, 1: 4, 2: 6, 3: 3, 4: 5, 5: 7, 6: 0, 7: 1}[f // 4]][1])
    for c in range(8):
        _seed(bX1B[c], bQK[3] if c < 4 else bQK[7])
    for q_ in range(4):
        _seed(bPQS[q_], bEP[1][q_ // 2])
        _seed(bZBS[q_], bEP[0][q_ // 2], bQNS[q_ // 2])

    def ln_block(tc, out_x1b, ps_s, bps_s, ps_q, bps_q, mt, bmt, rt, brt,
                 scr0, bscr0, scr1, bscr1):
        zb = (ZBS[0][:, :], ZBS[1][:, :])
        sq = (ZBS[2][:, :], ZBS[3][:, :])
        for oc in range(8):
            zsl = ZX[oc][:, tc * 512:(tc + 1) * 512]
            kb.op("dve", lambda v, d=zb[oc % 2], s=zsl: v.tensor_copy(d, s),
                  reads=[bZX[oc]], writes=[bZBS[oc % 2]])
            kb.op("act", lambda s_, d=sq[oc % 2], s=zsl: s_.activation(d, s, AF.Square),
                  reads=[bZX[oc]], writes=[bZBS[2 + oc % 2]])
            kb.op("pe", lambda t, o=ps_s[:, 0:512], r=zb[oc % 2], s=(oc == 0), e=(oc == 7):
                  t.matmul(o, ONES[:, :], r, start=s, stop=e, tile_position=(0, 0)),
                  reads=[bCONST, bZBS[oc % 2]], writes=[bps_s])
            kb.op("pe", lambda t, o=ps_q[:, 0:512], r=sq[oc % 2], s=(oc == 0), e=(oc == 7):
                  t.matmul(o, ONES[:, :], r, start=s, stop=e, tile_position=(0, 0)),
                  reads=[bCONST, bZBS[2 + oc % 2]], writes=[bps_q])
        kb.op("dve", lambda v: v.tensor_scalar(mt[:, :], ps_s[:, 0:512], 1.0 / 1024.0,
                                               None, ALU.mult),
              reads=[bps_s], writes=[bmt])
        kb.op("dve", lambda v: v.tensor_mul(scr0[:, :], mt[:, :], mt[:, :]),
              reads=[bmt], writes=[bscr0])
        kb.op("dve", lambda v: v.scalar_tensor_tensor(rt[:, :], ps_q[:, 0:512], 1.0 / 1024.0,
                                                      scr0[:, :], ALU.mult, ALU.subtract),
              reads=[bps_q, bscr0], writes=[brt])
        kb.op("act", lambda s: s.activation(scr0[:, :], rt[:, :], AF.Ln, bias=LSC[:, 4:5]),
              reads=[brt, bLSC], writes=[bscr0])
        kb.op("act", lambda s: s.activation(rt[:, :], scr0[:, :], AF.Exp, scale=-0.5),
              reads=[bscr0], writes=[brt])
        for oc in range(8):
            zsl = ZX[oc][:, tc * 512:(tc + 1) * 512]
            scr, bscr = (scr0, bscr0) if oc % 2 == 0 else (scr1, bscr1)
            kb.op("dve", lambda v, z=zsl, d=scr[:, :], m=mt[:, :]: v.tensor_sub(d, z, m),
                  reads=[bZX[oc], bmt], writes=[bscr])
            kb.op("dve", lambda v, z=zsl, d=scr[:, :], r=rt[:, :]: v.tensor_mul(z, d, r),
                  reads=[bscr, brt, bZX[oc]], writes=[bZX[oc]])
            if out_x1b:
                kb.op("pool", lambda g, d=X1B[oc][:, :], z=zsl: g.tensor_copy(d, z),
                      reads=[bZX[oc]], writes=[bX1B[oc]])

    ln_block(0, True, PS6, bPS6, PS7, bPS7, MT, bMT, RT, bRT,
             USC, bUSC, BSB, bBSB)
    for tc in range(NTC):
        for rc in range(2):
            for dc in range(8):
                lhs = U1S[:, dc * 256 + rc * 128: dc * 256 + (rc + 1) * 128]
                kb.op("pe", lambda t, o=PS4[:, :], l=lhs, r=X1B[dc][:, :],
                      s=(dc == 0), e=(dc == 7):
                      t.matmul(o, l, r, start=s, stop=e, tile_position=(0, 0)),
                      reads=[bW["u1"], bX1B[dc]], writes=[bPS4])
            kb.op("act", lambda s, d=PQS[:, rc * 512:(rc + 1) * 512], ps=PS4[:, :]:
                  s.copy(d, ps), reads=[bPS4], writes=[bPQS[rc]])
        for fc in range(32):
            hps, bhps = (PS5, bPS5) if fc % 2 == 0 else (PS4, bPS4)
            for rc in range(2):
                lhs = V1S[:, rc * 4096 + fc * 128: rc * 4096 + (fc + 1) * 128]
                rhs = PQS[:, rc * 512:(rc + 1) * 512]
                kb.op("pe", lambda t, o=hps[:, :], l=lhs, r=rhs,
                      s=(rc == 0), e=(rc == 1):
                      t.matmul(o, l, r, start=s, stop=e, tile_position=(0, 0)),
                      reads=[bW["v1"], bPQS[rc]], writes=[bhps])
            kb.op("act", lambda s, d=HT[fc][:, :], ps=hps[:, :], b=B1C[:, fc:fc + 1]:
                  s.activation(d, ps, AF.Gelu_apprx_tanh, bias=b),
                  reads=[bhps, bCONST], writes=[bHT[fc]])
        if tc + 1 < NTC:
            ln_block(tc + 1, True, PS6, bPS6, PS7, bPS7, MT, bMT, RT, bRT,
                     USC, bUSC, BSB, bBSB)
        for rc in range(2):
            for fc in range(32):
                lhs = U2S[:, fc * 256 + rc * 128: fc * 256 + (rc + 1) * 128]
                kb.op("pe", lambda t, o=PS7[:, :], l=lhs, r=HT[fc][:, :],
                      s=(fc == 0), e=(fc == 31):
                      t.matmul(o, l, r, start=s, stop=e, tile_position=(0, 0)),
                      reads=[bW["u2"], bHT[fc]], writes=[bPS7])
            kb.op("act", lambda s, d=PQS[:, 1024 + rc * 512: 1024 + (rc + 1) * 512],
                  ps=PS7[:, :]: s.copy(d, ps),
                  reads=[bPS7], writes=[bPQS[2 + rc]])
        for oc in range(NDC):
            ops_, bops = (PSA, bPSA) if oc % 2 == 0 else (PSB, bPSB)
            for rc in range(2):
                lhs = V2S[:, rc * 1024 + oc * 128: rc * 1024 + (oc + 1) * 128]
                rhs = PQS[:, 1024 + rc * 512: 1024 + (rc + 1) * 512]
                kb.op("pe", lambda t, o=ops_[:, 0:512], l=lhs, r=rhs,
                      s=(rc == 0), e=(rc == 1):
                      t.matmul(o, l, r, start=s, stop=e, tile_position=(0, 0)),
                      reads=[bW["v2"], bPQS[2 + rc]], writes=[bops])
            kb.op("dve", lambda v, d=ZX[oc][:, tc * 512:(tc + 1) * 512],
                  ps=ops_[:, 0:512], b=B2C[:, oc:oc + 1]:
                  v.scalar_tensor_tensor(d, ps, b, d, ALU.add, ALU.add),
                  reads=[bops, bCONST, bZX[oc]], writes=[bZX[oc]])
        ln_block(tc, False, PS5, bPS5, PS4, bPS4, MT2, bMT2, RT2, bRT2,
                 X0T[0], bX0[0], X0T[1], bX0[1])
        for oc in range(NDC):
            kb.dma("sp", "dSP", lambda g, oc=oc, tc=tc: g.dma_start(
                out=d_out[oc * 128:(oc + 1) * 128, tc * 512:(tc + 1) * 512],
                in_=ZX[oc][:, tc * 512:(tc + 1) * 512]),
                reads=[bZX[oc]])

    return _finish(nc, kb)


def _finish(nc, kb):
    kb.freeze()
    with nc.Block() as block:
        @block.sync
        def _(sync):
            kb.replay("sp", sync)
            for i in range(kb.npool["dSP"]):
                sync.wait_ge(kb.semh[f"dSP{i}"], kb.newtot[f"dSP{i}"])

        @block.tensor
        def _(tensor):
            kb.replay("pe", tensor)

        @block.scalar
        def _(scalar):
            kb.replay("act", scalar)

        @block.vector
        def _(vector):
            kb.replay("dve", vector)

        @block.gpsimd
        def _(gpsimd):
            kb.replay("pool", gpsimd)
    return nc


def _finish_debug(nc, kb, QK, VSB, YEX, bQK, bV, bYEX, ZX=None, bZX=None, nqk=16):
    d_dbg_qk = nc.dram_tensor("dbg_qk", [128, 16 * T], BF16, kind="ExternalOutput").ap()
    extra = []
    for b in bQK + bV + bYEX + (bZX or []):
        extra += list(b.w.items())
        extra += list(b.r.items())
    for i in range(nqk):
        kb.dma("sp", "dSP", lambda g, i=i: g.dma_start(
            out=d_dbg_qk[:, i * T:(i + 1) * T], in_=QK[i][:, :]), extra=extra)
    if VSB is not None:
        d_dbg_v = nc.dram_tensor("dbg_v", [128, NT * 1024], BF16, kind="ExternalOutput").ap()
        kb.dma("sp", "dSP", lambda g: g.dma_start(out=d_dbg_v, in_=VSB[:, :]), extra=extra)
    if YEX is not None:
        d_dbg_yex = nc.dram_tensor("dbg_yex", [128, 2 * T], BF16, kind="ExternalOutput").ap()
        for i in range(2):
            kb.dma("sp", "dSP", lambda g, i=i: g.dma_start(
                out=d_dbg_yex[:, i * T:(i + 1) * T], in_=YEX[i][:, :]), extra=extra)
    if ZX is not None:
        d_dbg_z = nc.dram_tensor("dbg_z", [128, 8 * T], F32, kind="ExternalOutput").ap()
        for i in range(8):
            kb.dma("sp", "dSP", lambda g, i=i: g.dma_start(
                out=d_dbg_z[:, i * T:(i + 1) * T], in_=ZX[i][:, :]), extra=extra)
    return _finish(nc, kb)


def _rotary_tables():
    import jax
    with jax.default_device(jax.devices("cpu")[0]):
        import jax.numpy as jnp
        inv_freq = 1.0 / 10000.0 ** (jnp.arange(0, 64, 2, dtype=jnp.float32) / 64)
        freqs = jnp.outer(jnp.arange(T, dtype=jnp.float32), inv_freq)
        return np.asarray(jnp.cos(freqs)), np.asarray(jnp.sin(freqs))


def stage_inputs(L0_attn_x, L0_attn_Wq, L0_attn_Wk, L0_attn_Wv, L0_attn_Wproj,
                 L0_attn_lambda_q1, L0_attn_lambda_k1, L0_attn_lambda_q2,
                 L0_attn_lambda_k2, L0_ffn_U1, L0_ffn_V1, L0_ffn_b1,
                 L0_ffn_U2, L0_ffn_V2, L0_ffn_b2):
    cos, sin = _rotary_tables()

    cos_t = cos.reshape(16, 128, 32).transpose(1, 0, 2)
    sin_t = sin.reshape(16, 128, 32).transpose(1, 0, 2)
    abt = np.concatenate([cos_t, cos_t, sin_t, -sin_t], axis=2)
    abt = np.ascontiguousarray(abt.reshape(128, 16 * 128)).astype(np.float32)

    s_i, t_i = np.arange(128)[:, None], np.arange(128)[None, :]
    lamr = np.concatenate([
        np.broadcast_to(np.asarray(v, np.float32)[None, :], (128, 32))
        for v in (L0_attn_lambda_q1, L0_attn_lambda_k1, L0_attn_lambda_q2, L0_attn_lambda_k2)
    ], axis=1)
    shared = {
        "wqT": np.ascontiguousarray(np.asarray(L0_attn_Wq, np.float32).T),
        "wkT": np.ascontiguousarray(np.asarray(L0_attn_Wk, np.float32).T),
        "wvT": np.ascontiguousarray(np.asarray(L0_attn_Wv, np.float32).T),
        "wpT": np.ascontiguousarray(np.asarray(L0_attn_Wproj, np.float32).T),
        "u1": np.ascontiguousarray(np.asarray(L0_ffn_U1, np.float32)),
        "v1f": np.ascontiguousarray(np.asarray(L0_ffn_V1, np.float32)),
        "u2": np.ascontiguousarray(np.asarray(L0_ffn_U2, np.float32)),
        "v2f": np.ascontiguousarray(np.asarray(L0_ffn_V2, np.float32)),
        "b1c": np.ascontiguousarray(np.asarray(L0_ffn_b1, np.float32).reshape(32, 128).T),
        "b2c": np.ascontiguousarray(np.asarray(L0_ffn_b2, np.float32).reshape(8, 128).T),
        "lamr": np.ascontiguousarray(lamr.astype(np.float32)),
        "abtT": abt,
        "maskb": np.ascontiguousarray((t_i >= s_i).astype(ml_dtypes.bfloat16)),
        "identb": np.ascontiguousarray(np.eye(128, dtype=ml_dtypes.bfloat16)),
    }
    x = np.asarray(L0_attn_x, np.float32)
    maps = []
    for b in range(8):
        xb = x[b]
        x1p = xb.reshape(16, 128, 8, 128).transpose(3, 0, 2, 1).reshape(128, -1)
        maps.append(dict(shared, xT=np.ascontiguousarray(xb.T),
                         x1p=np.ascontiguousarray(x1p)))
    return maps


def kernel(L0_attn_x, L0_attn_Wq, L0_attn_Wk, L0_attn_Wv, L0_attn_Wproj,
           L0_attn_lambda_q1, L0_attn_lambda_k1, L0_attn_lambda_q2,
           L0_attn_lambda_k2, L0_ffn_U1, L0_ffn_V1, L0_ffn_b1,
           L0_ffn_U2, L0_ffn_V2, L0_ffn_b2):
    in_maps = stage_inputs(L0_attn_x, L0_attn_Wq, L0_attn_Wk, L0_attn_Wv,
                           L0_attn_Wproj, L0_attn_lambda_q1, L0_attn_lambda_k1,
                           L0_attn_lambda_q2, L0_attn_lambda_k2, L0_ffn_U1,
                           L0_ffn_V1, L0_ffn_b1, L0_ffn_U2, L0_ffn_V2, L0_ffn_b2)
    nc = build_core()
    res = run_bass_kernel_spmd(nc, in_maps, core_ids=list(range(8)))
    out = np.stack([np.ascontiguousarray(r["outT"].T) for r in res.results])
    return out.astype(np.float32)


# version 35
# speedup vs baseline: 1.0184x; 1.0063x over previous
import math
import os

import numpy as np
import ml_dtypes
import concourse.bass as bass
import concourse.mybir as mybir
from concourse.bass_utils import run_bass_kernel_spmd

F32 = mybir.dt.float32
BF16 = mybir.dt.bfloat16
AF = mybir.ActivationFunctionType
ALU = mybir.AluOpType
AXX = mybir.AxisListType.X

T = 2048
D = 1024
NT = 16
NDC = 8
NTC = 4
DFF = 4096
RANK = 256

EPS_RMS = 2.220446049250313e-16 if os.environ.get("BASS_SIM_FP64") == "1" else 1.1920929e-07
EPS_LN = 1e-5
LAMBDA_INIT = 0.8 - 0.6 * math.exp(-0.3 * 1)


class _Buf:
    __slots__ = ("w", "r")

    def __init__(self):
        self.w = {}
        self.r = {}


def _seed(dst, *srcs):
    for s in srcs:
        for k, v in s.w.items():
            dst.r[k] = max(dst.r.get(k, 0), v)
        for k, v in s.r.items():
            dst.r[k] = max(dst.r.get(k, 0), v)


class _KB:

    def __init__(self, nc):
        self.nc = nc
        self.steps = {e: [] for e in ("pe", "act", "dve", "pool", "sp")}
        self.semh = {}
        self.cnt = {}
        names = ["cPE", "cACT", "cDVE", "cPOOL"]
        names += [f"dGP{i}" for i in range(10)] + [f"dSP{i}" for i in range(4)]
        for s in names:
            self.semh[s] = nc.alloc_semaphore(s)
            self.cnt[s] = 0
        self.clk = {"pe": "cPE", "act": "cACT", "dve": "cDVE", "pool": "cPOOL"}
        self.rr = {"dGP": 0, "dSP": 0}
        self.npool = {"dGP": 10, "dSP": 4}

    def _emit(self, eng, fn, amount, sem, reads, writes, extra):
        waits = list(extra)
        for b in reads:
            waits += b.w.items()
        for b in writes:
            waits += b.w.items()
            waits += b.r.items()
        own = self.clk.get(eng) if eng == "pe" else None
        w = {}
        for s, v in waits:
            if s is None or s == own:
                continue
            w[s] = max(w.get(s, 0), v)
        self.cnt[sem] += amount
        ts = (sem, self.cnt[sem])
        self.steps[eng].append((w, fn, ts, amount))
        for b in reads:
            b.r[ts[0]] = max(b.r.get(ts[0], 0), ts[1])
        for b in writes:
            b.w = {ts[0]: ts[1]}
            b.r = {}
        return ts

    def op(self, eng, fn, reads=(), writes=(), extra=()):
        return self._emit(eng, fn, 1, self.clk[eng], reads, writes, extra)

    def dma(self, eng, queue, fn, reads=(), writes=(), extra=()):
        sem = f"{queue}{self.rr[queue]}"
        self.rr[queue] = (self.rr[queue] + 1) % self.npool[queue]
        extra = list(extra) + [(sem, self.cnt[sem])]
        return self._emit(eng, fn, 16, sem, reads, writes, extra)

    def freeze(self):
        awaited = {}
        for steps in self.steps.values():
            for w, fn, ts, amount in steps:
                for s, v in w.items():
                    if v > 0:
                        awaited.setdefault(s, set()).add(v)
        bysem = {}
        for steps in self.steps.values():
            for w, fn, ts, amount in steps:
                bysem.setdefault(ts[0], []).append((ts[1], amount))
        self.remap = {}
        self.newtot = dict(self.cnt)
        for s, lst in bysem.items():
            lst.sort()
            m = {}
            run = 0
            aw = awaited.get(s, ())
            for oldc, amount in lst:
                keep = amount == 16 or oldc in aw
                if keep:
                    run += amount
                    m[oldc] = run
                else:
                    m[oldc] = None
            self.remap[s] = m
            self.newtot[s] = run

    def replay(self, eng_key, eng_obj):
        hw = {}
        for w, fn, ts, amount in self.steps[eng_key]:
            for s, v in sorted(w.items()):
                if v == 0:
                    continue
                nv = self.remap[s][v]
                assert nv is not None
                if hw.get(s, 0) < nv:
                    eng_obj.wait_ge(self.semh[s], nv)
                    hw[s] = nv
            ins = fn(eng_obj)
            if self.remap[ts[0]][ts[1]] is not None:
                ins.then_inc(self.semh[ts[0]], amount)


def build_core(debug_stop=None):
    nc = bass.Bass("TRN2", target_bir_lowering=False)

    d_xT = nc.dram_tensor("xT", [D, T], F32, kind="ExternalInput").ap()
    d_x1 = nc.dram_tensor("x1p", [128, NT * 8 * 128], F32, kind="ExternalInput").ap()
    d_wq = nc.dram_tensor("wqT", [D, D], F32, kind="ExternalInput").ap()
    d_wk = nc.dram_tensor("wkT", [D, D], F32, kind="ExternalInput").ap()
    d_wv = nc.dram_tensor("wvT", [D, D], F32, kind="ExternalInput").ap()
    d_wp = nc.dram_tensor("wpT", [D, D], F32, kind="ExternalInput").ap()
    d_u1 = nc.dram_tensor("u1", [D, RANK], F32, kind="ExternalInput").ap()
    d_v1 = nc.dram_tensor("v1f", [RANK, DFF], F32, kind="ExternalInput").ap()
    d_u2 = nc.dram_tensor("u2", [DFF, RANK], F32, kind="ExternalInput").ap()
    d_v2 = nc.dram_tensor("v2f", [RANK, D], F32, kind="ExternalInput").ap()
    d_b1 = nc.dram_tensor("b1c", [128, 32], F32, kind="ExternalInput").ap()
    d_b2 = nc.dram_tensor("b2c", [128, 8], F32, kind="ExternalInput").ap()
    d_lam = nc.dram_tensor("lamr", [128, 128], F32, kind="ExternalInput").ap()
    d_abt = nc.dram_tensor("abtT", [128, NT * 128], F32, kind="ExternalInput").ap()
    d_mask = nc.dram_tensor("maskb", [128, 128], BF16, kind="ExternalInput").ap()
    d_ident = nc.dram_tensor("identb", [128, 128], BF16, kind="ExternalInput").ap()
    d_out = nc.dram_tensor("outT", [D, T], F32, kind="ExternalOutput").ap()

    off = [16512]
    addr = {}

    def place(name, shape, dtype, at=None):
        nbytes = int(np.prod(shape[1:])) * (2 if dtype == BF16 else 4)
        o = at if at is not None else off[0]
        h = nc.alloc_sbuf_tensor_at(name, shape, dtype, offset=o)
        addr[name] = o
        if at is None:
            off[0] += (nbytes + 63) & ~63
        return h

    XSTG = [place(f"xstg{i}", [128, 8 * 128], BF16) for i in range(4)]
    QK = [place(f"qk{i}", [128, T], BF16) for i in range(16)]
    VSB = place("vsb", [128, NT * 1024], BF16)
    YEX = [place(f"yex{i}", [128, T], BF16) for i in range(2)]
    QF = place("qf", [128, 1024], F32)
    SQ = place("sq", [128, 1024], F32)
    ROT = place("rot", [128, 2048], BF16)
    WQ = place("wq_sb", [128, 8 * 1024], BF16)
    WK = place("wk_sb", [128, 8 * 1024], BF16)
    WV = place("wv_sb", [128, 8 * 1024], BF16)
    WP = place("wp_sb", [128, 8 * 1024], BF16)
    ESA = place("esa", [128, 2048], BF16)
    BSB = place("bsb", [128, 512], F32)
    PQS = place("pqs", [128, 2048], BF16)
    USC = place("usc", [128, 512], F32)
    TTS = place("tts_sb", [128, 16 * 96], F32)
    MSK = place("mask_sb", [128, 128], BF16)
    IDT = place("ident_sb", [128, 128], BF16)
    ONES = place("ones_sb", [128, 128], BF16)
    B1C = place("b1c_sb", [128, 32], F32)
    B2C = place("b2c_sb", [128, 8], F32)
    LAMR = place("lamr_sb", [128, 128], F32)
    LSC = place("lsc", [128, 8], F32)
    MS = place("ms", [128, 16], F32)
    RS = place("rs", [128, 16], F32)
    MS2 = place("ms2", [128, 16], F32)
    RS2 = place("rs2", [128, 16], F32)
    assert off[0] <= 229344, off[0]
    MT = place("mt", [128, 512], F32, at=addr["tts_sb"])
    RT = place("rt", [128, 512], F32, at=addr["tts_sb"] + 2048)
    MT2 = place("mt2", [128, 512], F32, at=addr["tts_sb"] + 4096)
    RT2 = place("rt2", [128, 512], F32, at=addr["xstg2"])
    YR = [[place("yr00", [128, 512], F32, at=addr["tts_sb"]),
           place("yr01", [128, 512], F32, at=addr["tts_sb"] + 2048)],
          [place("yr10", [128, 512], F32, at=addr["tts_sb"] + 4096),
           place("yr11", [128, 512], F32, at=addr["xstg2"])]]
    DR = [[place("dr00", [128, 512], F32, at=addr["qf"]),
           place("dr01", [128, 512], F32, at=addr["qf"] + 2048)],
          [place("dr10", [128, 512], F32, at=addr["qf"] + 4096),
           place("dr11", [128, 512], F32, at=addr["qf"] + 6144)]]

    QB = place("qbn", [128, 1024], BF16, at=addr["yex0"])
    KB_ = place("kbn", [128, 1024], BF16, at=addr["yex0"] + 2048)
    SQK = place("sqk", [128, 1024], F32, at=addr["yex1"])
    QNS = [place("qns0", [128, 1024], BF16, at=addr["esa"]),
           place("qns1", [128, 1024], BF16, at=addr["esa"] + 2048)]
    WBS = [place("wbs0", [128, 1024], BF16, at=addr["pqs"]),
           place("wbs1", [128, 1024], BF16, at=addr["pqs"] + 2048)]
    ABT = place("abt_sb", [128, NT * 128], BF16, at=addr["rot"])
    RSE = [place("rse0", [128, 1024], BF16, at=addr["sq"]),
           place("rse1", [128, 1024], BF16, at=addr["sqk"])]
    X0T = []
    for i in range(8):
        at = addr["qf"] + 2048 * i if i < 6 else addr["xstg0"] + 2048 * (i - 6)
        X0T.append(place(f"x0t{i}", [128, 512], F32, at=at))
    X1B = [place(f"x1b{c}", [128, 512], BF16,
                 at=addr["qk3" if c < 4 else "qk7"] + 1024 * (c % 4)) for c in range(8)]
    _h_homes = ["qk0", "qk1", "qk2", "qk4", "qk5", "qk6", "yex0", "yex1"]
    HT = [place(f"ht{f}", [128, 512], BF16,
                at=addr[_h_homes[f // 4]] + 1024 * (f % 4)) for f in range(32)]
    ZX = [place(f"zx{c}", [128, T], F32,
                at=(addr["vsb"] + 8192 * c) if c < 4 else (addr[f"qk{8 + 2 * (c - 4)}"]))
          for c in range(8)]
    ZBS = [place(f"zbs{i}", [128, 512], BF16, at=addr["esa"] + 1024 * i)
           for i in range(4)]
    U1S = place("u1_sb", [128, 8 * 256], BF16, at=addr["wq_sb"])
    V2S = place("v2_sb", [128, 2 * 1024], BF16, at=addr["wq_sb"] + 4096)
    V1S = place("v1_sb", [128, 2 * 4096], BF16, at=addr["wk_sb"])
    U2S = place("u2_sb", [128, 32 * 256], BF16, at=addr["wv_sb"])

    PSA = nc.alloc_psum_tensor("psA", [128, 1024], F32)
    PSB = nc.alloc_psum_tensor("psB", [128, 1024], F32)
    PS4 = nc.alloc_psum_tensor("ps4", [128, 512], F32)
    PS5 = nc.alloc_psum_tensor("ps5", [128, 512], F32)
    PS6 = nc.alloc_psum_tensor("ps6", [128, 512], F32)
    PS7 = nc.alloc_psum_tensor("ps7", [128, 512], F32)

    kb = _KB(nc)

    bXS = [_Buf() for _ in range(4)]
    bQK = [_Buf() for _ in range(16)]
    bV = [_Buf() for _ in range(16)]
    bYEX = [_Buf() for _ in range(2)]
    bQF, bSQ, bQB, bKB, bSQK = _Buf(), _Buf(), _Buf(), _Buf(), _Buf()
    bQNS = [_Buf(), _Buf()]
    bWBS = [_Buf(), _Buf()]
    bABT = _Buf()
    bW = {k: _Buf() for k in ("u1", "v1", "u2", "v2")}
    bWc = {k: [_Buf() for _ in range(4)] for k in ("wq", "wk", "wv", "wp")}
    bEP = [[_Buf(), _Buf()], [_Buf(), _Buf()]]
    bBSB, bUSC = _Buf(), _Buf()
    bPQS = [_Buf() for _ in range(4)]
    bCONST = _Buf()
    bLSC = _Buf()
    bMS, bRS, bMS2, bRS2, bMT, bRT, bMT2, bRT2 = (_Buf() for _ in range(8))
    bYR = [[_Buf(), _Buf()], [_Buf(), _Buf()]]
    bDR = [[_Buf(), _Buf()], [_Buf(), _Buf()]]
    bPSA, bPSB, bPS4, bPS5, bPS6, bPS7 = (_Buf() for _ in range(6))
    bX0 = [_Buf() for _ in range(8)]
    bHT = [_Buf() for _ in range(32)]
    bZX = [_Buf() for _ in range(8)]
    bX1B = [_Buf() for _ in range(8)]
    bZBS = [_Buf() for _ in range(4)]

    def ld(dst_ap, src_ap):
        return lambda g: g.dma_start(out=dst_ap, in_=src_ap)

    def wload_chunk(dst, src, n, ch, bufs, extra=()):
        dv = dst[:, :].rearrange("p (c n) -> p c n", n=n)[:, 2 * ch:2 * ch + 2, :]
        sv = src.rearrange("(c p) n -> p c n", p=128)[:, 2 * ch:2 * ch + 2, :]
        kb.dma("pool", "dGP", lambda g, d=dv, s=sv: g.dma_start(out=d, in_=s),
               writes=[bufs[ch]], extra=extra)

    for ch in range(4):
        wload_chunk(WQ, d_wq, 1024, ch, bWc["wq"])
    kb.dma("pool", "dGP", ld(ABT[:, :], d_abt), writes=[bABT])
    kb.dma("sp", "dSP", ld(MSK[:, :], d_mask), writes=[bCONST])
    kb.dma("sp", "dSP", ld(IDT[:, :], d_ident), writes=[bCONST])
    kb.dma("sp", "dSP", ld(B1C[:, :], d_b1), writes=[bCONST])
    kb.dma("sp", "dSP", ld(B2C[:, :], d_b2), writes=[bCONST])
    ts_lam = kb.dma("sp", "dSP", ld(LAMR[:, :], d_lam), writes=[bCONST])

    wpend = [(WK, d_wk, ch, bWc["wk"]) for ch in range(4)]
    wpend += [(WV, d_wv, ch, bWc["wv"]) for ch in range(4)]
    wpend += [(WP, d_wp, ch, bWc["wp"]) for ch in range(4)]

    kb.op("dve", lambda v: v.memset(ONES[:, :], 1.0), writes=[bCONST])
    kb.op("dve", lambda v: v.memset(LSC[:, 3:4], float(EPS_RMS)), writes=[bLSC])
    kb.op("dve", lambda v: v.memset(LSC[:, 4:5], float(EPS_LN)), writes=[bLSC])

    kb.op("dve", lambda v: v.tensor_mul(LAMR[:, 0:32], LAMR[:, 0:32], LAMR[:, 32:64]),
          reads=[bCONST], writes=[bCONST], extra=[ts_lam])
    kb.op("dve", lambda v: v.tensor_mul(LAMR[:, 64:96], LAMR[:, 64:96], LAMR[:, 96:128]),
          reads=[bCONST], writes=[bCONST])
    kb.op("dve", lambda v: v.reduce_sum(LSC[:, 0:1], LAMR[:, 0:32], axis=AXX),
          reads=[bCONST], writes=[bLSC])
    kb.op("dve", lambda v: v.reduce_sum(LSC[:, 1:2], LAMR[:, 64:96], axis=AXX),
          reads=[bCONST], writes=[bLSC])
    kb.op("act", lambda s: s.activation(LSC[:, 0:1], LSC[:, 0:1], AF.Exp),
          reads=[bLSC], writes=[bLSC])
    kb.op("act", lambda s: s.activation(LSC[:, 1:2], LSC[:, 1:2], AF.Exp),
          reads=[bLSC], writes=[bLSC])
    kb.op("dve", lambda v: v.tensor_sub(LSC[:, 2:3], LSC[:, 0:1], LSC[:, 1:2]),
          reads=[bLSC], writes=[bLSC])
    kb.op("dve", lambda v: v.tensor_scalar(LSC[:, 2:3], LSC[:, 2:3],
                                           float(LAMBDA_INIT), -1.0, ALU.add, ALU.mult),
          reads=[bLSC], writes=[bLSC])

    def x_load(vt):
        i = vt % NT
        sl = vt % 4
        src = d_x1[:, i * 1024:(i + 1) * 1024]
        dst = XSTG[sl][:, :]
        kb.dma("pool", "dGP", lambda g, s=src, d=dst: g.dma_start(out=d, in_=s),
               writes=[bXS[sl]])
        if vt % 4 == 1 and wpend:
            dst_w, src_w, ch, bufs = wpend.pop(0)
            wload_chunk(dst_w, src_w, 1024, ch, bufs)

    for vt in range(3):
        x_load(vt)

    def qrf(i):
        return QK[8 + i // 2][:, (i % 2) * 1024:(i % 2 + 1) * 1024]

    def krf(i):
        return VSB[:, i * 1024:(i + 1) * 1024]

    tcount = [0]

    def emit_T(i, rfv, brf, slot0):
        _rot = ((PS6, bPS6), (PS7, bPS7), (PS4, bPS4), (PS5, bPS5))
        for fj in range(8):
            n = tcount[0]
            tcount[0] += 1
            ps, bps = _rot[n % 4]
            blk = rfv[:, fj * 128:(fj + 1) * 128]
            kb.op("pe", lambda t, o=ps[:, 0:128], l=blk:
                  t.matmul(o, l, IDT[:, :], start=True, stop=True, tile_position=(0, 0)),
                  reads=[brf, bCONST], writes=[bps])
            dst = QK[slot0 + fj][:, i * 128:(i + 1) * 128]
            if n % 2 == 0:
                kb.op("dve", lambda v, d=dst, s=ps[:, 0:128]: v.tensor_copy(d, s),
                      reads=[bps], writes=[bQK[slot0 + fj]])
            else:
                kb.op("act", lambda s_, d=dst, s=ps[:, 0:128]: s_.copy(d, s),
                      reads=[bps], writes=[bQK[slot0 + fj]])

    PSQv = ((PSA, bPSA), (PSB, bPSB))
    QBv, bQBv = (QB, KB_), (bQB, bKB)
    SQv, bSQv = (SQ, SQK), (bSQ, bSQK)
    MSv, bMSv = (MS, MS2), (bMS, bMS2)

    def emit_mms(vt, wsb, bwks):
        i, sl, par = vt % NT, vt % 4, vt % 2
        psq, bpsq = PSQv[par]
        for dc in range(NDC):
            lhs = XSTG[sl][:, dc * 128:(dc + 1) * 128]
            for half in (0, 1):
                rhs = wsb[:, dc * 1024 + half * 512: dc * 1024 + (half + 1) * 512]
                kb.op("pe", lambda t, o=psq[:, half * 512:(half + 1) * 512], l=lhs,
                      r=rhs, s=(dc == 0), e=(dc == NDC - 1):
                      t.matmul(o, l, r, start=s, stop=e, tile_position=(0, 0)),
                      reads=[bXS[sl], bwks[dc // 2]], writes=[bpsq])
        return psq, bpsq

    def emit_qk_pass(pidx, wsb, bwks, rf_of, brf_of, tp_of):
        for i in range(NT):
            par = i % 2
            psq, bpsq = emit_mms(pidx * NT + i, wsb, bwks)
            tpb = tp_of(i)
            if tpb is not None:
                emit_T(*tpb)
            qb, bqb = QBv[par], bQBv[par]
            sq, bsq = SQv[par], bSQv[par]
            ms, bms = MSv[par], bMSv[par]
            qns, bqns = QNS[par], bQNS[par]
            wb, bwb = WBS[par], bWBS[par]
            rse, brse = RSE[par], bSQv[par]
            rfv, brf = rf_of(i), brf_of(i)
            kb.op("act", lambda s, d=qb[:, :], p=psq[:, :]: s.copy(d, p),
                  reads=[bpsq], writes=[bqb])
            kb.op("act", lambda s, d=sq[:, :], p=psq[:, :]:
                  s.activation(d, p, AF.Square), reads=[bpsq], writes=[bsq])
            qv = qb[:, :].rearrange("p (h s c) -> p h s c", s=2, c=32)
            sv = qns[:, :].rearrange("p (h s c) -> p h s c", s=2, c=32)
            kb.op("pool", lambda g, d=sv[:, :, 0, :], s=qv[:, :, 1, :]:
                  g.tensor_copy(d, s), reads=[bqb], writes=[bqns])
            kb.op("pool", lambda g, d=sv[:, :, 1, :], s=qv[:, :, 0, :]:
                  g.tensor_copy(d, s), reads=[bqb], writes=[bqns])
            kb.op("dve", lambda v, o=ms[:, :], s=sq[:, :]:
                  v.reduce_sum(o, s.rearrange("p (s c) -> p s c", c=64), axis=AXX),
                  reads=[bsq], writes=[bms])
            kb.op("act", lambda s, m=ms[:, :]: s.activation(
                m, m, AF.Ln, bias=LSC[:, 3:4], scale=1.0 / 64.0),
                reads=[bms, bLSC], writes=[bms])
            rsev = rse[:, :].rearrange("p (s c) -> p s c", c=64)
            msb = ms[:, :].unsqueeze(2).broadcast_to([128, 16, 64])
            kb.op("act", lambda s, r=rsev, m=msb: s.activation(r, m, AF.Exp, scale=-0.5),
                  reads=[bms, bsq], writes=[brse])
            qn3 = qb[:, :].rearrange("p (s c) -> p s c", c=64)
            qs3 = qns[:, :].rearrange("p (s c) -> p s c", c=64)
            wb3 = wb[:, :].rearrange("p (s c) -> p s c", c=64)
            rf3 = rfv.rearrange("p (s c) -> p s c", c=64)
            ca = ABT[:, i * 128: i * 128 + 64].unsqueeze(1).broadcast_to([128, 16, 64])
            sa = ABT[:, i * 128 + 64: i * 128 + 128].unsqueeze(1).broadcast_to([128, 16, 64])
            kb.op("dve", lambda v, o=rf3, q=qn3, m=ca: v.tensor_mul(o, q, m),
                  reads=[bqb, bABT], writes=[brf])
            kb.op("dve", lambda v, o=wb3, q=qs3, m=sa: v.tensor_mul(o, q, m),
                  reads=[bqns, bABT], writes=[bwb])
            kb.op("dve", lambda v, o=qns[:, :], a=rfv, b=wb[:, :]: v.tensor_add(o, a, b),
                  reads=[bwb, brf, bqns], writes=[bqns])
            kb.op("dve", lambda v, o=rfv, a=qns[:, :], r=rse[:, :]: v.tensor_mul(o, a, r),
                  reads=[brse, bqns, brf], writes=[brf])
            if pidx * NT + i + 3 < 48:
                x_load(pidx * NT + i + 3)

    emit_qk_pass(0, WQ, bWc["wq"], qrf, lambda i: bQK[8 + i // 2], lambda i: None)
    emit_qk_pass(1, WK, bWc["wk"], krf, lambda i: bV[i],
                 lambda i: (i, qrf(i), bQK[8 + i // 2], 0))
    for i in range(NT):
        emit_T(i, krf(i), bV[i], 8)
        psq, bpsq = emit_mms(2 * NT + i, WV, bWc["wv"])
        kb.op("act", lambda s, d=VSB[:, i * 1024:(i + 1) * 1024], p=psq[:, :]:
              s.copy(d, p), reads=[bpsq], writes=[bV[i]])
        if 2 * NT + i + 3 < 48:
            x_load(2 * NT + i + 3)

    if debug_stop == "qkgen":
        return _finish_debug(nc, kb, QK, VSB, None, bQK, bV, bYEX)

    def warena(*keys):
        ex = []
        for k in keys:
            for b in bWc[k]:
                ex += list(b.w.items()) + list(b.r.items())
        return ex

    def wload(dst, src, n):
        return lambda g: g.dma_start(
            out=dst[:, :].rearrange("p (c n) -> p c n", n=n),
            in_=src.rearrange("(c p) n -> p c n", p=128))

    kb.dma("pool", "dGP", wload(V1S, d_v1, 4096), writes=[bW["v1"]],
           extra=warena("wk"))
    kb.dma("pool", "dGP", wload(U2S, d_u2, 256), writes=[bW["u2"]],
           extra=warena("wv"))
    kb.dma("pool", "dGP", wload(U1S, d_u1, 256), writes=[bW["u1"]],
           extra=warena("wq"))
    kb.dma("pool", "dGP", wload(V2S, d_v2, 1024), writes=[bW["v2"]],
           extra=warena("wq"))

    ycomb = {0: (YEX[0], bYEX[0]), 1: (YEX[1], bYEX[1]),
             2: (QK[0], bQK[0]), 3: (QK[4], bQK[4]),
             4: (QK[1], bQK[1]), 5: (QK[5], bQK[5]),
             6: (QK[2], bQK[2]), 7: (QK[6], bQK[6])}
    EA2 = place("ea2", [128, 1024], BF16, at=addr["xstg0"])
    EB2 = place("eb2", [128, 1024], BF16, at=addr["rot"])
    EP = {0: (ESA[:, 0:1024], ESA[:, 1024:2048], EA2[:, :]),
          1: (PQS[:, 0:1024], PQS[:, 1024:2048], EB2[:, :])}
    for j in range(2):
        _seed(bEP[0][j], bQNS[j])
        _seed(bEP[1][j], bWBS[j])
    bEP[0].append(_Buf())
    bEP[1].append(_Buf())
    _seed(bEP[0][2], bXS[0], bXS[1])
    _seed(bEP[1][2], bABT)
    _seed(bYEX[0], bQB)
    _seed(bYEX[1], bKB)
    _seed(bYEX[0], bKB)
    _seed(bYEX[1], bSQK)

    _seed(bYR[0][0], bCONST)
    _seed(bYR[0][1], bCONST)
    _seed(bYR[1][0], bXS[2])
    _seed(bYR[1][1], bXS[3])
    _seed(bDR[0][0], bQF)
    _seed(bDR[0][1], bQF)
    _seed(bDR[1][0], bSQ)
    _seed(bDR[1][1], bSQ)

    SPS = {0: (PSA, bPSA), 1: (PSB, bPSB)}
    YPS = {0: (PS4, bPS4), 1: (PS5, bPS5)}
    DPS = {0: (PS6, bPS6), 1: (PS7, bPS7)}

    pending = []

    def flush_pending():
        for eng, fn, reads, writes in pending:
            kb.op(eng, fn, reads=reads, writes=writes)
        pending.clear()

    for h in range(8):
        p = h // 2
        pbase = 64 * (h % 2)

        def emit_avden(pr, tc, h=h):
            st_last = 4 * (tc + 1) - 1
            for g in (0, 1):
                yps, byps = YPS[g]
                dps, bdps = DPS[g]
                ep, bep = EP[g][pr % 3], bEP[g][pr % 3]
                for k_ in (0, 1):
                    st = 2 * pr + k_
                    e = ep[:, k_ * 512:(k_ + 1) * 512]
                    vsl = VSB[:, st * 1024 + h * 128: st * 1024 + (h + 1) * 128]
                    kb.op("pe", lambda t, o=yps[:, :], l=vsl, e_=e,
                          s=(st == 0), en=(st == st_last):
                          t.matmul(o, l, e_, start=s, stop=en, tile_position=(0, 0)),
                          reads=[bV[st], bep], writes=[byps])
                    kb.op("pe", lambda t, o=dps[:, :], e_=e,
                          s=(st == 0), en=(st == st_last):
                          t.matmul(o, ONES[:, :], e_, start=s, stop=en,
                                   tile_position=(0, 0)),
                          reads=[bCONST, bep], writes=[bdps])

        for tc in range(NTC):
            npr = 2 * (tc + 1)
            for pr in range(npr):
                st0 = 2 * pr
                for g in (0, 1):
                    sps, bsps = SPS[g]
                    kslot, qslot = 8 + g * 4 + p, g * 4 + p
                    for k_ in (0, 1):
                        st = st0 + k_
                        lhs = QK[kslot][pbase:pbase + 64, st * 128:(st + 1) * 128]
                        rhs = QK[qslot][pbase:pbase + 64, tc * 512:(tc + 1) * 512]
                        kb.op("pe", lambda t, o=sps[:, k_ * 512:(k_ + 1) * 512],
                              l=lhs, r=rhs, tp=(pbase, 0):
                              t.matmul(o, l, r, start=True, stop=True, tile_position=tp),
                              reads=[bQK[kslot], bQK[qslot]], writes=[bsps])
                for g in (0, 1):
                    sps, bsps = SPS[g]
                    ep, bep = EP[g][pr % 3], bEP[g][pr % 3]
                    kb.op("act", lambda s, e=ep, ps=sps[:, :]:
                          s.activation(e, ps, AF.Exp, scale=0.125),
                          reads=[bsps], writes=[bep])
                for k_ in (0, 1):
                    st = st0 + k_
                    if st >= 4 * tc:
                        w = (st - 4 * tc) * 128
                        o = k_ * 512
                        for g in (0, 1):
                            ep, bep = EP[g][pr % 3], bEP[g][pr % 3]
                            if w > 0:
                                kb.op("dve", lambda v, e=ep[:, o:o + w]: v.memset(e, 0.0),
                                      writes=[bep])
                            kb.op("dve", lambda v, e=ep[:, o + w:o + w + 128]:
                                  v.tensor_mul(e, e, MSK[:, :]),
                                  reads=[bCONST], writes=[bep])
                if pr > 0:
                    emit_avden(pr - 1, tc)
                if pr == 1:
                    flush_pending()
            emit_avden(npr - 1, tc)

            par = tc % 2
            ysl, bysl = ycomb[h]
            dst = ysl[:, tc * 512:(tc + 1) * 512]
            for g in (0, 1):
                yps, byps = YPS[g]
                kb.op("dve", lambda v, d=YR[g][par][:, :], ps=yps[:, :]:
                      v.tensor_copy(d, ps), reads=[byps], writes=[bYR[g][par]])
            for g in (0, 1):
                dps, bdps = DPS[g]
                kb.op("dve", lambda v, d=DR[g][par][:, :], ps=dps[:, :]:
                      v.tensor_copy(d, ps), reads=[bdps], writes=[bDR[g][par]])
            for g in (0, 1):
                pending.append(("act",
                                (lambda s, dr=DR[g][par][:, :]:
                                 s.activation(USC[:, :], dr, AF.Ln)),
                                [bDR[g][par]], [bUSC]))
                pending.append(("act",
                                (lambda s: s.activation(BSB[:, :], USC[:, :],
                                                        AF.Exp, scale=-1.0)),
                                [bUSC], [bBSB]))
                if g == 0:
                    pending.append(("dve",
                                    (lambda v, d=dst, yr=YR[0][par][:, :]:
                                     v.tensor_mul(d, yr, BSB[:, :])),
                                    [bYR[0][par], bBSB], [bysl]))
                else:
                    pending.append(("dve",
                                    (lambda v: v.tensor_scalar(BSB[:, :], BSB[:, :],
                                                               LSC[:, 2:3], None, ALU.mult)),
                                    [bLSC, bBSB], [bBSB]))
                    pending.append(("dve",
                                    (lambda v, yr=YR[1][par][:, :]:
                                     v.tensor_mul(USC[:, :], yr, BSB[:, :])),
                                    [bYR[1][par], bBSB], [bUSC]))
                    pending.append(("dve",
                                    (lambda v, d=dst: v.tensor_add(d, d, USC[:, :])),
                                    [bUSC, bysl], [bysl]))

    flush_pending()

    if debug_stop == "attn":
        return _finish_debug(nc, kb, QK, VSB, YEX, bQK, bV, bYEX)

    for c in range(8):
        _seed(bZX[c], *(bV[4 * c:4 * c + 4] if c < 4
                        else [bQK[8 + 2 * (c - 4)], bQK[9 + 2 * (c - 4)]]))
    _x0src = [bDR[0][0], bDR[0][1], bDR[1][0], bDR[1][1],
              bABT, bABT, bXS[0], bXS[1]]
    for i in range(8):
        _seed(bX0[i], _x0src[i])
    _seed(bX0[4], bEP[1][2])
    _seed(bX0[5], bEP[1][2])
    _seed(bX0[6], bEP[0][2])
    _seed(bMT, bYR[0][0])
    _seed(bRT, bYR[0][1])
    _seed(bMT2, bYR[1][0])
    _seed(bRT2, bYR[1][1])

    def x0_load(tc):
        for c in range(8):
            kb.dma("sp", "dSP", lambda g, c=c, tc=tc: g.dma_start(
                out=X0T[c][:, :],
                in_=d_xT[c * 128:(c + 1) * 128, tc * 512:(tc + 1) * 512]),
                writes=[bX0[c]])

    x0_load(0)
    PRJ = {0: (PS4, bPS4), 1: (PS5, bPS5)}
    for tc in range(NTC):
        for oc in range(NDC):
            prj, bprj = PRJ[oc % 2]
            for kc in range(8):
                ysl, bysl = ycomb[kc]
                lhs = WP[:, kc * 1024 + oc * 128: kc * 1024 + (oc + 1) * 128]
                rhs = ysl[:, tc * 512:(tc + 1) * 512]
                kb.op("pe", lambda t, o=prj[:, :], l=lhs, r=rhs,
                      s=(kc == 0), e=(kc == 7):
                      t.matmul(o, l, r, start=s, stop=e, tile_position=(0, 0)),
                      reads=[bWc["wp"][kc // 2], bysl], writes=[bprj])
            kb.op("dve", lambda v, d=ZX[oc][:, tc * 512:(tc + 1) * 512],
                  ps=prj[:, :], x0=X0T[oc][:, :]: v.tensor_add(d, ps, x0),
                  reads=[bprj, bX0[oc]], writes=[bZX[oc]])
        if tc + 1 < NTC:
            x0_load(tc + 1)

    if debug_stop == "proj":
        return _finish_debug(nc, kb, QK, None, YEX, bQK, bV, bYEX, ZX=ZX, bZX=bZX, nqk=8)

    for f in range(32):
        _seed(bHT[f], ycomb[{0: 2, 1: 4, 2: 6, 3: 3, 4: 5, 5: 7, 6: 0, 7: 1}[f // 4]][1])
    for c in range(8):
        _seed(bX1B[c], bQK[3] if c < 4 else bQK[7])
    for q_ in range(4):
        _seed(bPQS[q_], bEP[1][q_ // 2])
        _seed(bZBS[q_], bEP[0][q_ // 2], bQNS[q_ // 2])

    def ln_block(tc, out_x1b, ps_s, bps_s, ps_q, bps_q, mt, bmt, rt, brt,
                 scr0, bscr0, scr1, bscr1):
        zb = (ZBS[0][:, :], ZBS[1][:, :])
        sq = (ZBS[2][:, :], ZBS[3][:, :])
        for oc in range(8):
            zsl = ZX[oc][:, tc * 512:(tc + 1) * 512]
            kb.op("dve", lambda v, d=zb[oc % 2], s=zsl: v.tensor_copy(d, s),
                  reads=[bZX[oc]], writes=[bZBS[oc % 2]])
            if out_x1b:
                kb.op("dve", lambda v, d=sq[oc % 2], s=zsl: v.tensor_mul(d, s, s),
                      reads=[bZX[oc]], writes=[bZBS[2 + oc % 2]])
            else:
                kb.op("act", lambda s_, d=sq[oc % 2], s=zsl:
                      s_.activation(d, s, AF.Square),
                      reads=[bZX[oc]], writes=[bZBS[2 + oc % 2]])
            kb.op("pe", lambda t, o=ps_s[:, 0:512], r=zb[oc % 2], s=(oc == 0), e=(oc == 7):
                  t.matmul(o, ONES[:, :], r, start=s, stop=e, tile_position=(0, 0)),
                  reads=[bCONST, bZBS[oc % 2]], writes=[bps_s])
            kb.op("pe", lambda t, o=ps_q[:, 0:512], r=sq[oc % 2], s=(oc == 0), e=(oc == 7):
                  t.matmul(o, ONES[:, :], r, start=s, stop=e, tile_position=(0, 0)),
                  reads=[bCONST, bZBS[2 + oc % 2]], writes=[bps_q])
        kb.op("dve", lambda v: v.tensor_scalar(mt[:, :], ps_s[:, 0:512], 1.0 / 1024.0,
                                               None, ALU.mult),
              reads=[bps_s], writes=[bmt])
        kb.op("dve", lambda v: v.tensor_mul(scr0[:, :], mt[:, :], mt[:, :]),
              reads=[bmt], writes=[bscr0])
        kb.op("dve", lambda v: v.scalar_tensor_tensor(rt[:, :], ps_q[:, 0:512], 1.0 / 1024.0,
                                                      scr0[:, :], ALU.mult, ALU.subtract),
              reads=[bps_q, bscr0], writes=[brt])
        kb.op("act", lambda s: s.activation(scr0[:, :], rt[:, :], AF.Ln, bias=LSC[:, 4:5]),
              reads=[brt, bLSC], writes=[bscr0])
        kb.op("act", lambda s: s.activation(rt[:, :], scr0[:, :], AF.Exp, scale=-0.5),
              reads=[bscr0], writes=[brt])
        for oc in range(8):
            zsl = ZX[oc][:, tc * 512:(tc + 1) * 512]
            scr, bscr = (scr0, bscr0) if oc % 2 == 0 else (scr1, bscr1)
            kb.op("dve", lambda v, z=zsl, d=scr[:, :], m=mt[:, :]: v.tensor_sub(d, z, m),
                  reads=[bZX[oc], bmt], writes=[bscr])
            kb.op("dve", lambda v, z=zsl, d=scr[:, :], r=rt[:, :]: v.tensor_mul(z, d, r),
                  reads=[bscr, brt, bZX[oc]], writes=[bZX[oc]])
            if out_x1b:
                kb.op("pool", lambda g, d=X1B[oc][:, :], z=zsl: g.tensor_copy(d, z),
                      reads=[bZX[oc]], writes=[bX1B[oc]])

    ln_block(0, True, PS6, bPS6, PS7, bPS7, MT, bMT, RT, bRT,
             USC, bUSC, BSB, bBSB)
    for tc in range(NTC):
        for rc in range(2):
            for dc in range(8):
                lhs = U1S[:, dc * 256 + rc * 128: dc * 256 + (rc + 1) * 128]
                kb.op("pe", lambda t, o=PS4[:, :], l=lhs, r=X1B[dc][:, :],
                      s=(dc == 0), e=(dc == 7):
                      t.matmul(o, l, r, start=s, stop=e, tile_position=(0, 0)),
                      reads=[bW["u1"], bX1B[dc]], writes=[bPS4])
            kb.op("act", lambda s, d=PQS[:, rc * 512:(rc + 1) * 512], ps=PS4[:, :]:
                  s.copy(d, ps), reads=[bPS4], writes=[bPQS[rc]])
        for fc in range(32):
            hps, bhps = (PS5, bPS5) if fc % 2 == 0 else (PS4, bPS4)
            for rc in range(2):
                lhs = V1S[:, rc * 4096 + fc * 128: rc * 4096 + (fc + 1) * 128]
                rhs = PQS[:, rc * 512:(rc + 1) * 512]
                kb.op("pe", lambda t, o=hps[:, :], l=lhs, r=rhs,
                      s=(rc == 0), e=(rc == 1):
                      t.matmul(o, l, r, start=s, stop=e, tile_position=(0, 0)),
                      reads=[bW["v1"], bPQS[rc]], writes=[bhps])
            kb.op("act", lambda s, d=HT[fc][:, :], ps=hps[:, :], b=B1C[:, fc:fc + 1]:
                  s.activation(d, ps, AF.Gelu_apprx_tanh, bias=b),
                  reads=[bhps, bCONST], writes=[bHT[fc]])
        if tc + 1 < NTC:
            ln_block(tc + 1, True, PS6, bPS6, PS7, bPS7, MT, bMT, RT, bRT,
                     USC, bUSC, BSB, bBSB)
        for rc in range(2):
            for fc in range(32):
                lhs = U2S[:, fc * 256 + rc * 128: fc * 256 + (rc + 1) * 128]
                kb.op("pe", lambda t, o=PS7[:, :], l=lhs, r=HT[fc][:, :],
                      s=(fc == 0), e=(fc == 31):
                      t.matmul(o, l, r, start=s, stop=e, tile_position=(0, 0)),
                      reads=[bW["u2"], bHT[fc]], writes=[bPS7])
            kb.op("act", lambda s, d=PQS[:, 1024 + rc * 512: 1024 + (rc + 1) * 512],
                  ps=PS7[:, :]: s.copy(d, ps),
                  reads=[bPS7], writes=[bPQS[2 + rc]])
        for oc in range(NDC):
            ops_, bops = (PSA, bPSA) if oc % 2 == 0 else (PSB, bPSB)
            for rc in range(2):
                lhs = V2S[:, rc * 1024 + oc * 128: rc * 1024 + (oc + 1) * 128]
                rhs = PQS[:, 1024 + rc * 512: 1024 + (rc + 1) * 512]
                kb.op("pe", lambda t, o=ops_[:, 0:512], l=lhs, r=rhs,
                      s=(rc == 0), e=(rc == 1):
                      t.matmul(o, l, r, start=s, stop=e, tile_position=(0, 0)),
                      reads=[bW["v2"], bPQS[2 + rc]], writes=[bops])
            kb.op("dve", lambda v, d=ZX[oc][:, tc * 512:(tc + 1) * 512],
                  ps=ops_[:, 0:512], b=B2C[:, oc:oc + 1]:
                  v.scalar_tensor_tensor(d, ps, b, d, ALU.add, ALU.add),
                  reads=[bops, bCONST, bZX[oc]], writes=[bZX[oc]])
        ln_block(tc, False, PS5, bPS5, PS4, bPS4, MT2, bMT2, RT2, bRT2,
                 X0T[0], bX0[0], X0T[1], bX0[1])
        for oc in range(NDC):
            kb.dma("sp", "dSP", lambda g, oc=oc, tc=tc: g.dma_start(
                out=d_out[oc * 128:(oc + 1) * 128, tc * 512:(tc + 1) * 512],
                in_=ZX[oc][:, tc * 512:(tc + 1) * 512]),
                reads=[bZX[oc]])

    return _finish(nc, kb)


def _finish(nc, kb):
    kb.freeze()
    with nc.Block() as block:
        @block.sync
        def _(sync):
            kb.replay("sp", sync)
            for i in range(kb.npool["dSP"]):
                sync.wait_ge(kb.semh[f"dSP{i}"], kb.newtot[f"dSP{i}"])

        @block.tensor
        def _(tensor):
            kb.replay("pe", tensor)

        @block.scalar
        def _(scalar):
            kb.replay("act", scalar)

        @block.vector
        def _(vector):
            kb.replay("dve", vector)

        @block.gpsimd
        def _(gpsimd):
            kb.replay("pool", gpsimd)
    return nc


def _finish_debug(nc, kb, QK, VSB, YEX, bQK, bV, bYEX, ZX=None, bZX=None, nqk=16):
    d_dbg_qk = nc.dram_tensor("dbg_qk", [128, 16 * T], BF16, kind="ExternalOutput").ap()
    extra = []
    for b in bQK + bV + bYEX + (bZX or []):
        extra += list(b.w.items())
        extra += list(b.r.items())
    for i in range(nqk):
        kb.dma("sp", "dSP", lambda g, i=i: g.dma_start(
            out=d_dbg_qk[:, i * T:(i + 1) * T], in_=QK[i][:, :]), extra=extra)
    if VSB is not None:
        d_dbg_v = nc.dram_tensor("dbg_v", [128, NT * 1024], BF16, kind="ExternalOutput").ap()
        kb.dma("sp", "dSP", lambda g: g.dma_start(out=d_dbg_v, in_=VSB[:, :]), extra=extra)
    if YEX is not None:
        d_dbg_yex = nc.dram_tensor("dbg_yex", [128, 2 * T], BF16, kind="ExternalOutput").ap()
        for i in range(2):
            kb.dma("sp", "dSP", lambda g, i=i: g.dma_start(
                out=d_dbg_yex[:, i * T:(i + 1) * T], in_=YEX[i][:, :]), extra=extra)
    if ZX is not None:
        d_dbg_z = nc.dram_tensor("dbg_z", [128, 8 * T], F32, kind="ExternalOutput").ap()
        for i in range(8):
            kb.dma("sp", "dSP", lambda g, i=i: g.dma_start(
                out=d_dbg_z[:, i * T:(i + 1) * T], in_=ZX[i][:, :]), extra=extra)
    return _finish(nc, kb)


def _rotary_tables():
    import jax
    with jax.default_device(jax.devices("cpu")[0]):
        import jax.numpy as jnp
        inv_freq = 1.0 / 10000.0 ** (jnp.arange(0, 64, 2, dtype=jnp.float32) / 64)
        freqs = jnp.outer(jnp.arange(T, dtype=jnp.float32), inv_freq)
        return np.asarray(jnp.cos(freqs)), np.asarray(jnp.sin(freqs))


def stage_inputs(L0_attn_x, L0_attn_Wq, L0_attn_Wk, L0_attn_Wv, L0_attn_Wproj,
                 L0_attn_lambda_q1, L0_attn_lambda_k1, L0_attn_lambda_q2,
                 L0_attn_lambda_k2, L0_ffn_U1, L0_ffn_V1, L0_ffn_b1,
                 L0_ffn_U2, L0_ffn_V2, L0_ffn_b2):
    cos, sin = _rotary_tables()

    cos_t = cos.reshape(16, 128, 32).transpose(1, 0, 2)
    sin_t = sin.reshape(16, 128, 32).transpose(1, 0, 2)
    abt = np.concatenate([cos_t, cos_t, sin_t, -sin_t], axis=2)
    abt = np.ascontiguousarray(abt.reshape(128, 16 * 128)).astype(np.float32)

    s_i, t_i = np.arange(128)[:, None], np.arange(128)[None, :]
    lamr = np.concatenate([
        np.broadcast_to(np.asarray(v, np.float32)[None, :], (128, 32))
        for v in (L0_attn_lambda_q1, L0_attn_lambda_k1, L0_attn_lambda_q2, L0_attn_lambda_k2)
    ], axis=1)
    shared = {
        "wqT": np.ascontiguousarray(np.asarray(L0_attn_Wq, np.float32).T),
        "wkT": np.ascontiguousarray(np.asarray(L0_attn_Wk, np.float32).T),
        "wvT": np.ascontiguousarray(np.asarray(L0_attn_Wv, np.float32).T),
        "wpT": np.ascontiguousarray(np.asarray(L0_attn_Wproj, np.float32).T),
        "u1": np.ascontiguousarray(np.asarray(L0_ffn_U1, np.float32)),
        "v1f": np.ascontiguousarray(np.asarray(L0_ffn_V1, np.float32)),
        "u2": np.ascontiguousarray(np.asarray(L0_ffn_U2, np.float32)),
        "v2f": np.ascontiguousarray(np.asarray(L0_ffn_V2, np.float32)),
        "b1c": np.ascontiguousarray(np.asarray(L0_ffn_b1, np.float32).reshape(32, 128).T),
        "b2c": np.ascontiguousarray(np.asarray(L0_ffn_b2, np.float32).reshape(8, 128).T),
        "lamr": np.ascontiguousarray(lamr.astype(np.float32)),
        "abtT": abt,
        "maskb": np.ascontiguousarray((t_i >= s_i).astype(ml_dtypes.bfloat16)),
        "identb": np.ascontiguousarray(np.eye(128, dtype=ml_dtypes.bfloat16)),
    }
    x = np.asarray(L0_attn_x, np.float32)
    maps = []
    for b in range(8):
        xb = x[b]
        x1p = xb.reshape(16, 128, 8, 128).transpose(3, 0, 2, 1).reshape(128, -1)
        maps.append(dict(shared, xT=np.ascontiguousarray(xb.T),
                         x1p=np.ascontiguousarray(x1p)))
    return maps


def kernel(L0_attn_x, L0_attn_Wq, L0_attn_Wk, L0_attn_Wv, L0_attn_Wproj,
           L0_attn_lambda_q1, L0_attn_lambda_k1, L0_attn_lambda_q2,
           L0_attn_lambda_k2, L0_ffn_U1, L0_ffn_V1, L0_ffn_b1,
           L0_ffn_U2, L0_ffn_V2, L0_ffn_b2):
    in_maps = stage_inputs(L0_attn_x, L0_attn_Wq, L0_attn_Wk, L0_attn_Wv,
                           L0_attn_Wproj, L0_attn_lambda_q1, L0_attn_lambda_k1,
                           L0_attn_lambda_q2, L0_attn_lambda_k2, L0_ffn_U1,
                           L0_ffn_V1, L0_ffn_b1, L0_ffn_U2, L0_ffn_V2, L0_ffn_b2)
    nc = build_core()
    res = run_bass_kernel_spmd(nc, in_maps, core_ids=list(range(8)))
    out = np.stack([np.ascontiguousarray(r["outT"].T) for r in res.results])
    return out.astype(np.float32)


# version 37 (final)
# speedup vs baseline: 1.0194x; 1.0010x over previous
import math
import os

import numpy as np
import ml_dtypes
import concourse.bass as bass
import concourse.mybir as mybir
from concourse.bass_utils import run_bass_kernel_spmd

F32 = mybir.dt.float32
BF16 = mybir.dt.bfloat16
AF = mybir.ActivationFunctionType
ALU = mybir.AluOpType
AXX = mybir.AxisListType.X

T = 2048
D = 1024
NT = 16
NDC = 8
NTC = 4
DFF = 4096
RANK = 256

EPS_RMS = 2.220446049250313e-16 if os.environ.get("BASS_SIM_FP64") == "1" else 1.1920929e-07
EPS_LN = 1e-5
LAMBDA_INIT = 0.8 - 0.6 * math.exp(-0.3 * 1)


class _Buf:
    __slots__ = ("w", "r")

    def __init__(self):
        self.w = {}
        self.r = {}


def _seed(dst, *srcs):
    for s in srcs:
        for k, v in s.w.items():
            dst.r[k] = max(dst.r.get(k, 0), v)
        for k, v in s.r.items():
            dst.r[k] = max(dst.r.get(k, 0), v)


class _KB:

    def __init__(self, nc):
        self.nc = nc
        self.steps = {e: [] for e in ("pe", "act", "dve", "pool", "sp")}
        self.semh = {}
        self.cnt = {}
        names = ["cPE", "cACT", "cDVE", "cPOOL"]
        names += [f"dGP{i}" for i in range(10)] + [f"dSP{i}" for i in range(4)]
        for s in names:
            self.semh[s] = nc.alloc_semaphore(s)
            self.cnt[s] = 0
        self.clk = {"pe": "cPE", "act": "cACT", "dve": "cDVE", "pool": "cPOOL"}
        self.rr = {"dGP": 0, "dSP": 0}
        self.npool = {"dGP": 10, "dSP": 4}

    def _emit(self, eng, fn, amount, sem, reads, writes, extra):
        waits = list(extra)
        for b in reads:
            waits += b.w.items()
        for b in writes:
            waits += b.w.items()
            waits += b.r.items()
        own = self.clk.get(eng) if eng == "pe" else None
        w = {}
        for s, v in waits:
            if s is None or s == own:
                continue
            w[s] = max(w.get(s, 0), v)
        self.cnt[sem] += amount
        ts = (sem, self.cnt[sem])
        self.steps[eng].append((w, fn, ts, amount))
        for b in reads:
            b.r[ts[0]] = max(b.r.get(ts[0], 0), ts[1])
        for b in writes:
            b.w = {ts[0]: ts[1]}
            b.r = {}
        return ts

    def op(self, eng, fn, reads=(), writes=(), extra=()):
        return self._emit(eng, fn, 1, self.clk[eng], reads, writes, extra)

    def dma(self, eng, queue, fn, reads=(), writes=(), extra=()):
        sem = f"{queue}{self.rr[queue]}"
        self.rr[queue] = (self.rr[queue] + 1) % self.npool[queue]
        extra = list(extra) + [(sem, self.cnt[sem])]
        return self._emit(eng, fn, 16, sem, reads, writes, extra)

    def freeze(self):
        awaited = {}
        for steps in self.steps.values():
            for w, fn, ts, amount in steps:
                for s, v in w.items():
                    if v > 0:
                        awaited.setdefault(s, set()).add(v)
        bysem = {}
        for steps in self.steps.values():
            for w, fn, ts, amount in steps:
                bysem.setdefault(ts[0], []).append((ts[1], amount))
        self.remap = {}
        self.newtot = dict(self.cnt)
        for s, lst in bysem.items():
            lst.sort()
            m = {}
            run = 0
            aw = awaited.get(s, ())
            for oldc, amount in lst:
                keep = amount == 16 or oldc in aw
                if keep:
                    run += amount
                    m[oldc] = run
                else:
                    m[oldc] = None
            self.remap[s] = m
            self.newtot[s] = run

    def replay(self, eng_key, eng_obj):
        hw = {}
        for w, fn, ts, amount in self.steps[eng_key]:
            for s, v in sorted(w.items()):
                if v == 0:
                    continue
                nv = self.remap[s][v]
                assert nv is not None
                if hw.get(s, 0) < nv:
                    eng_obj.wait_ge(self.semh[s], nv)
                    hw[s] = nv
            ins = fn(eng_obj)
            if self.remap[ts[0]][ts[1]] is not None:
                ins.then_inc(self.semh[ts[0]], amount)


def build_core(debug_stop=None):
    nc = bass.Bass("TRN2", target_bir_lowering=False)

    d_xT = nc.dram_tensor("xT", [D, T], F32, kind="ExternalInput").ap()
    d_x1 = nc.dram_tensor("x1p", [128, NT * 8 * 128], F32, kind="ExternalInput").ap()
    d_wq = nc.dram_tensor("wqT", [D, D], F32, kind="ExternalInput").ap()
    d_wk = nc.dram_tensor("wkT", [D, D], F32, kind="ExternalInput").ap()
    d_wv = nc.dram_tensor("wvT", [D, D], F32, kind="ExternalInput").ap()
    d_wp = nc.dram_tensor("wpT", [D, D], F32, kind="ExternalInput").ap()
    d_u1 = nc.dram_tensor("u1", [D, RANK], F32, kind="ExternalInput").ap()
    d_v1 = nc.dram_tensor("v1f", [RANK, DFF], F32, kind="ExternalInput").ap()
    d_u2 = nc.dram_tensor("u2", [DFF, RANK], F32, kind="ExternalInput").ap()
    d_v2 = nc.dram_tensor("v2f", [RANK, D], F32, kind="ExternalInput").ap()
    d_b1 = nc.dram_tensor("b1c", [128, 32], F32, kind="ExternalInput").ap()
    d_b2 = nc.dram_tensor("b2c", [128, 8], F32, kind="ExternalInput").ap()
    d_lam = nc.dram_tensor("lamr", [128, 128], F32, kind="ExternalInput").ap()
    d_abt = nc.dram_tensor("abtT", [128, NT * 128], F32, kind="ExternalInput").ap()
    d_mask = nc.dram_tensor("maskb", [128, 128], BF16, kind="ExternalInput").ap()
    d_ident = nc.dram_tensor("identb", [128, 128], BF16, kind="ExternalInput").ap()
    d_out = nc.dram_tensor("outT", [D, T], F32, kind="ExternalOutput").ap()

    off = [16512]
    addr = {}

    def place(name, shape, dtype, at=None):
        nbytes = int(np.prod(shape[1:])) * (2 if dtype == BF16 else 4)
        o = at if at is not None else off[0]
        h = nc.alloc_sbuf_tensor_at(name, shape, dtype, offset=o)
        addr[name] = o
        if at is None:
            off[0] += (nbytes + 63) & ~63
        return h

    XSTG = [place(f"xstg{i}", [128, 8 * 128], BF16) for i in range(4)]
    QK = [place(f"qk{i}", [128, T], BF16) for i in range(16)]
    VSB = place("vsb", [128, NT * 1024], BF16)
    YEX = [place(f"yex{i}", [128, T], BF16) for i in range(2)]
    QF = place("qf", [128, 1024], F32)
    SQ = place("sq", [128, 1024], F32)
    ROT = place("rot", [128, 2048], BF16)
    WQ = place("wq_sb", [128, 8 * 1024], BF16)
    WK = place("wk_sb", [128, 8 * 1024], BF16)
    WV = place("wv_sb", [128, 8 * 1024], BF16)
    WP = place("wp_sb", [128, 8 * 1024], BF16)
    ESA = place("esa", [128, 2048], BF16)
    BSB = place("bsb", [128, 512], F32)
    PQS = place("pqs", [128, 2048], BF16)
    USC = place("usc", [128, 512], F32)
    TTS = place("tts_sb", [128, 16 * 96], F32)
    MSK = place("mask_sb", [128, 128], BF16)
    IDT = place("ident_sb", [128, 128], BF16)
    ONES = place("ones_sb", [128, 128], BF16)
    B1C = place("b1c_sb", [128, 32], F32)
    B2C = place("b2c_sb", [128, 8], F32)
    LAMR = place("lamr_sb", [128, 128], F32)
    LSC = place("lsc", [128, 8], F32)
    MS = place("ms", [128, 16], F32)
    RS = place("rs", [128, 16], F32)
    MS2 = place("ms2", [128, 16], F32)
    RS2 = place("rs2", [128, 16], F32)
    assert off[0] <= 229344, off[0]
    MT = place("mt", [128, 512], F32, at=addr["tts_sb"])
    RT = place("rt", [128, 512], F32, at=addr["tts_sb"] + 2048)
    MT2 = place("mt2", [128, 512], F32, at=addr["tts_sb"] + 4096)
    RT2 = place("rt2", [128, 512], F32, at=addr["xstg2"])
    YR = [[place("yr00", [128, 512], F32, at=addr["tts_sb"]),
           place("yr01", [128, 512], F32, at=addr["tts_sb"] + 2048)],
          [place("yr10", [128, 512], F32, at=addr["tts_sb"] + 4096),
           place("yr11", [128, 512], F32, at=addr["xstg2"])]]
    DR = [[place("dr00", [128, 512], F32, at=addr["qf"]),
           place("dr01", [128, 512], F32, at=addr["qf"] + 2048)],
          [place("dr10", [128, 512], F32, at=addr["qf"] + 4096),
           place("dr11", [128, 512], F32, at=addr["qf"] + 6144)]]

    QB = place("qbn", [128, 1024], BF16, at=addr["yex0"])
    KB_ = place("kbn", [128, 1024], BF16, at=addr["yex0"] + 2048)
    SQK = place("sqk", [128, 1024], F32, at=addr["yex1"])
    QNS = [place("qns0", [128, 1024], BF16, at=addr["esa"]),
           place("qns1", [128, 1024], BF16, at=addr["esa"] + 2048)]
    WBS = [place("wbs0", [128, 1024], BF16, at=addr["pqs"]),
           place("wbs1", [128, 1024], BF16, at=addr["pqs"] + 2048)]
    ABT = place("abt_sb", [128, NT * 128], BF16, at=addr["rot"])
    RSE = [place("rse0", [128, 1024], BF16, at=addr["sq"]),
           place("rse1", [128, 1024], BF16, at=addr["sqk"])]
    X0T = []
    for i in range(8):
        at = addr["qf"] + 2048 * i if i < 6 else addr["xstg0"] + 2048 * (i - 6)
        X0T.append(place(f"x0t{i}", [128, 512], F32, at=at))
    X1B = [place(f"x1b{c}", [128, 512], BF16,
                 at=addr["qk3" if c < 4 else "qk7"] + 1024 * (c % 4)) for c in range(8)]
    _h_homes = ["qk0", "qk1", "qk2", "qk4", "qk5", "qk6", "yex0", "yex1"]
    HT = [place(f"ht{f}", [128, 512], BF16,
                at=addr[_h_homes[f // 4]] + 1024 * (f % 4)) for f in range(32)]
    ZX = [place(f"zx{c}", [128, T], F32,
                at=(addr["vsb"] + 8192 * c) if c < 4 else (addr[f"qk{8 + 2 * (c - 4)}"]))
          for c in range(8)]
    ZBS = [place(f"zbs{i}", [128, 512], BF16, at=addr["esa"] + 1024 * i)
           for i in range(4)]
    U1S = place("u1_sb", [128, 8 * 256], BF16, at=addr["wq_sb"])
    V2S = place("v2_sb", [128, 2 * 1024], BF16, at=addr["wq_sb"] + 4096)
    V1S = place("v1_sb", [128, 2 * 4096], BF16, at=addr["wk_sb"])
    U2S = place("u2_sb", [128, 32 * 256], BF16, at=addr["wv_sb"])

    PSA = nc.alloc_psum_tensor("psA", [128, 1024], F32)
    PSB = nc.alloc_psum_tensor("psB", [128, 1024], F32)
    PS4 = nc.alloc_psum_tensor("ps4", [128, 512], F32)
    PS5 = nc.alloc_psum_tensor("ps5", [128, 512], F32)
    PS6 = nc.alloc_psum_tensor("ps6", [128, 512], F32)
    PS7 = nc.alloc_psum_tensor("ps7", [128, 512], F32)

    kb = _KB(nc)

    bXS = [_Buf() for _ in range(4)]
    bQK = [_Buf() for _ in range(16)]
    bV = [_Buf() for _ in range(16)]
    bYEX = [_Buf() for _ in range(2)]
    bQF, bSQ, bQB, bKB, bSQK = _Buf(), _Buf(), _Buf(), _Buf(), _Buf()
    bQNS = [_Buf(), _Buf()]
    bWBS = [_Buf(), _Buf()]
    bABT = _Buf()
    bW = {k: _Buf() for k in ("u1", "v1", "u2", "v2")}
    bWc = {k: [_Buf() for _ in range(4)] for k in ("wq", "wk", "wv", "wp")}
    bEP = [[_Buf(), _Buf()], [_Buf(), _Buf()]]
    bBSB, bUSC = _Buf(), _Buf()
    bPQS = [_Buf() for _ in range(4)]
    bCONST = _Buf()
    bLSC = _Buf()
    bMS, bRS, bMS2, bRS2, bMT, bRT, bMT2, bRT2 = (_Buf() for _ in range(8))
    bYR = [[_Buf(), _Buf()], [_Buf(), _Buf()]]
    bDR = [[_Buf(), _Buf()], [_Buf(), _Buf()]]
    bPSA, bPSB, bPS4, bPS5, bPS6, bPS7 = (_Buf() for _ in range(6))
    bX0 = [_Buf() for _ in range(8)]
    bHT = [_Buf() for _ in range(32)]
    bZX = [_Buf() for _ in range(8)]
    bX1B = [_Buf() for _ in range(8)]
    bZBS = [_Buf() for _ in range(4)]

    def ld(dst_ap, src_ap):
        return lambda g: g.dma_start(out=dst_ap, in_=src_ap)

    def wload_chunk(dst, src, n, ch, bufs, extra=()):
        dv = dst[:, :].rearrange("p (c n) -> p c n", n=n)[:, 2 * ch:2 * ch + 2, :]
        sv = src.rearrange("(c p) n -> p c n", p=128)[:, 2 * ch:2 * ch + 2, :]
        kb.dma("pool", "dGP", lambda g, d=dv, s=sv: g.dma_start(out=d, in_=s),
               writes=[bufs[ch]], extra=extra)

    for ch in range(4):
        wload_chunk(WQ, d_wq, 1024, ch, bWc["wq"])
    kb.dma("pool", "dGP", ld(ABT[:, :], d_abt), writes=[bABT])
    kb.dma("sp", "dSP", ld(MSK[:, :], d_mask), writes=[bCONST])
    kb.dma("sp", "dSP", ld(IDT[:, :], d_ident), writes=[bCONST])
    kb.dma("sp", "dSP", ld(B1C[:, :], d_b1), writes=[bCONST])
    kb.dma("sp", "dSP", ld(B2C[:, :], d_b2), writes=[bCONST])
    ts_lam = kb.dma("sp", "dSP", ld(LAMR[:, :], d_lam), writes=[bCONST])

    wpend = [(WK, d_wk, ch, bWc["wk"]) for ch in range(4)]
    wpend += [(WV, d_wv, ch, bWc["wv"]) for ch in range(4)]
    wpend += [(WP, d_wp, ch, bWc["wp"]) for ch in range(4)]

    kb.op("dve", lambda v: v.memset(ONES[:, :], 1.0), writes=[bCONST])
    kb.op("dve", lambda v: v.memset(LSC[:, 3:4], float(EPS_RMS)), writes=[bLSC])
    kb.op("dve", lambda v: v.memset(LSC[:, 4:5], float(EPS_LN)), writes=[bLSC])

    kb.op("dve", lambda v: v.tensor_mul(LAMR[:, 0:32], LAMR[:, 0:32], LAMR[:, 32:64]),
          reads=[bCONST], writes=[bCONST], extra=[ts_lam])
    kb.op("dve", lambda v: v.tensor_mul(LAMR[:, 64:96], LAMR[:, 64:96], LAMR[:, 96:128]),
          reads=[bCONST], writes=[bCONST])
    kb.op("dve", lambda v: v.reduce_sum(LSC[:, 0:1], LAMR[:, 0:32], axis=AXX),
          reads=[bCONST], writes=[bLSC])
    kb.op("dve", lambda v: v.reduce_sum(LSC[:, 1:2], LAMR[:, 64:96], axis=AXX),
          reads=[bCONST], writes=[bLSC])
    kb.op("act", lambda s: s.activation(LSC[:, 0:1], LSC[:, 0:1], AF.Exp),
          reads=[bLSC], writes=[bLSC])
    kb.op("act", lambda s: s.activation(LSC[:, 1:2], LSC[:, 1:2], AF.Exp),
          reads=[bLSC], writes=[bLSC])
    kb.op("dve", lambda v: v.tensor_sub(LSC[:, 2:3], LSC[:, 0:1], LSC[:, 1:2]),
          reads=[bLSC], writes=[bLSC])
    kb.op("dve", lambda v: v.tensor_scalar(LSC[:, 2:3], LSC[:, 2:3],
                                           float(LAMBDA_INIT), -1.0, ALU.add, ALU.mult),
          reads=[bLSC], writes=[bLSC])

    def x_load(vt):
        i = vt % NT
        sl = vt % 4
        src = d_x1[:, i * 1024:(i + 1) * 1024]
        dst = XSTG[sl][:, :]
        kb.dma("pool", "dGP", lambda g, s=src, d=dst: g.dma_start(out=d, in_=s),
               writes=[bXS[sl]])
        if vt % 4 == 1 and wpend:
            dst_w, src_w, ch, bufs = wpend.pop(0)
            wload_chunk(dst_w, src_w, 1024, ch, bufs)

    for vt in range(3):
        x_load(vt)

    def qrf(i):
        return QK[8 + i // 2][:, (i % 2) * 1024:(i % 2 + 1) * 1024]

    def krf(i):
        return VSB[:, i * 1024:(i + 1) * 1024]

    tcount = [0]

    def emit_T(i, rfv, brf, slot0):
        _rot = ((PS6, bPS6), (PS7, bPS7), (PS4, bPS4), (PS5, bPS5))
        for fj in range(8):
            n = tcount[0]
            tcount[0] += 1
            ps, bps = _rot[n % 4]
            blk = rfv[:, fj * 128:(fj + 1) * 128]
            kb.op("pe", lambda t, o=ps[:, 0:128], l=blk:
                  t.matmul(o, l, IDT[:, :], start=True, stop=True, tile_position=(0, 0)),
                  reads=[brf, bCONST], writes=[bps])
            dst = QK[slot0 + fj][:, i * 128:(i + 1) * 128]
            if n % 2 == 0:
                kb.op("dve", lambda v, d=dst, s=ps[:, 0:128]: v.tensor_copy(d, s),
                      reads=[bps], writes=[bQK[slot0 + fj]])
            else:
                kb.op("act", lambda s_, d=dst, s=ps[:, 0:128]: s_.copy(d, s),
                      reads=[bps], writes=[bQK[slot0 + fj]])

    PSQv = ((PSA, bPSA), (PSB, bPSB))
    QBv, bQBv = (QB, KB_), (bQB, bKB)
    SQv, bSQv = (SQ, SQK), (bSQ, bSQK)
    MSv, bMSv = (MS, MS2), (bMS, bMS2)

    def emit_mms(vt, wsb, bwks):
        i, sl, par = vt % NT, vt % 4, vt % 2
        psq, bpsq = PSQv[par]
        for dc in range(NDC):
            lhs = XSTG[sl][:, dc * 128:(dc + 1) * 128]
            for half in (0, 1):
                rhs = wsb[:, dc * 1024 + half * 512: dc * 1024 + (half + 1) * 512]
                kb.op("pe", lambda t, o=psq[:, half * 512:(half + 1) * 512], l=lhs,
                      r=rhs, s=(dc == 0), e=(dc == NDC - 1):
                      t.matmul(o, l, r, start=s, stop=e, tile_position=(0, 0)),
                      reads=[bXS[sl], bwks[dc // 2]], writes=[bpsq])
        return psq, bpsq

    def emit_qk_pass(pidx, wsb, bwks, rf_of, brf_of, tp_of):
        for i in range(NT):
            par = i % 2
            psq, bpsq = emit_mms(pidx * NT + i, wsb, bwks)
            tpb = tp_of(i)
            if tpb is not None:
                emit_T(*tpb)
            qb, bqb = QBv[par], bQBv[par]
            sq, bsq = SQv[par], bSQv[par]
            ms, bms = MSv[par], bMSv[par]
            qns, bqns = QNS[par], bQNS[par]
            wb, bwb = WBS[par], bWBS[par]
            rse, brse = RSE[par], bSQv[par]
            rfv, brf = rf_of(i), brf_of(i)
            kb.op("act", lambda s, d=qb[:, :], p=psq[:, :]: s.copy(d, p),
                  reads=[bpsq], writes=[bqb])
            kb.op("act", lambda s, d=sq[:, :], p=psq[:, :]:
                  s.activation(d, p, AF.Square), reads=[bpsq], writes=[bsq])
            qv = qb[:, :].rearrange("p (h s c) -> p h s c", s=2, c=32)
            sv = qns[:, :].rearrange("p (h s c) -> p h s c", s=2, c=32)
            kb.op("pool", lambda g, d=sv[:, :, 0, :], s=qv[:, :, 1, :]:
                  g.tensor_copy(d, s), reads=[bqb], writes=[bqns])
            kb.op("pool", lambda g, d=sv[:, :, 1, :], s=qv[:, :, 0, :]:
                  g.tensor_copy(d, s), reads=[bqb], writes=[bqns])
            kb.op("dve", lambda v, o=ms[:, :], s=sq[:, :]:
                  v.reduce_sum(o, s.rearrange("p (s c) -> p s c", c=64), axis=AXX),
                  reads=[bsq], writes=[bms])
            kb.op("act", lambda s, m=ms[:, :]: s.activation(
                m, m, AF.Ln, bias=LSC[:, 3:4], scale=1.0 / 64.0),
                reads=[bms, bLSC], writes=[bms])
            rsev = rse[:, :].rearrange("p (s c) -> p s c", c=64)
            msb = ms[:, :].unsqueeze(2).broadcast_to([128, 16, 64])
            kb.op("act", lambda s, r=rsev, m=msb: s.activation(r, m, AF.Exp, scale=-0.5),
                  reads=[bms, bsq], writes=[brse])
            qn3 = qb[:, :].rearrange("p (s c) -> p s c", c=64)
            qs3 = qns[:, :].rearrange("p (s c) -> p s c", c=64)
            wb3 = wb[:, :].rearrange("p (s c) -> p s c", c=64)
            rf3 = rfv.rearrange("p (s c) -> p s c", c=64)
            ca = ABT[:, i * 128: i * 128 + 64].unsqueeze(1).broadcast_to([128, 16, 64])
            sa = ABT[:, i * 128 + 64: i * 128 + 128].unsqueeze(1).broadcast_to([128, 16, 64])
            kb.op("dve", lambda v, o=rf3, q=qn3, m=ca: v.tensor_mul(o, q, m),
                  reads=[bqb, bABT], writes=[brf])
            kb.op("dve", lambda v, o=wb3, q=qs3, m=sa: v.tensor_mul(o, q, m),
                  reads=[bqns, bABT], writes=[bwb])
            kb.op("dve", lambda v, o=qns[:, :], a=rfv, b=wb[:, :]: v.tensor_add(o, a, b),
                  reads=[bwb, brf, bqns], writes=[bqns])
            kb.op("dve", lambda v, o=rfv, a=qns[:, :], r=rse[:, :]: v.tensor_mul(o, a, r),
                  reads=[brse, bqns, brf], writes=[brf])
            if pidx * NT + i + 3 < 48:
                x_load(pidx * NT + i + 3)

    emit_qk_pass(0, WQ, bWc["wq"], qrf, lambda i: bQK[8 + i // 2], lambda i: None)
    emit_qk_pass(1, WK, bWc["wk"], krf, lambda i: bV[i],
                 lambda i: (i, qrf(i), bQK[8 + i // 2], 0))
    for i in range(NT):
        emit_T(i, krf(i), bV[i], 8)
        psq, bpsq = emit_mms(2 * NT + i, WV, bWc["wv"])
        kb.op("act", lambda s, d=VSB[:, i * 1024:(i + 1) * 1024], p=psq[:, :]:
              s.copy(d, p), reads=[bpsq], writes=[bV[i]])
        if 2 * NT + i + 3 < 48:
            x_load(2 * NT + i + 3)

    if debug_stop == "qkgen":
        return _finish_debug(nc, kb, QK, VSB, None, bQK, bV, bYEX)

    def warena(*keys):
        ex = []
        for k in keys:
            for b in bWc[k]:
                ex += list(b.w.items()) + list(b.r.items())
        return ex

    def wload(dst, src, n):
        return lambda g: g.dma_start(
            out=dst[:, :].rearrange("p (c n) -> p c n", n=n),
            in_=src.rearrange("(c p) n -> p c n", p=128))

    kb.dma("pool", "dGP", wload(V1S, d_v1, 4096), writes=[bW["v1"]],
           extra=warena("wk"))
    kb.dma("pool", "dGP", wload(U2S, d_u2, 256), writes=[bW["u2"]],
           extra=warena("wv"))
    kb.dma("pool", "dGP", wload(U1S, d_u1, 256), writes=[bW["u1"]],
           extra=warena("wq"))
    kb.dma("pool", "dGP", wload(V2S, d_v2, 1024), writes=[bW["v2"]],
           extra=warena("wq"))

    ycomb = {0: (YEX[0], bYEX[0]), 1: (YEX[1], bYEX[1]),
             2: (QK[0], bQK[0]), 3: (QK[4], bQK[4]),
             4: (QK[1], bQK[1]), 5: (QK[5], bQK[5]),
             6: (QK[2], bQK[2]), 7: (QK[6], bQK[6])}
    EA2 = place("ea2", [128, 1024], BF16, at=addr["xstg0"])
    EB2 = place("eb2", [128, 1024], BF16, at=addr["rot"])
    EP = {0: (ESA[:, 0:1024], ESA[:, 1024:2048], EA2[:, :]),
          1: (PQS[:, 0:1024], PQS[:, 1024:2048], EB2[:, :])}
    for j in range(2):
        _seed(bEP[0][j], bQNS[j])
        _seed(bEP[1][j], bWBS[j])
    bEP[0].append(_Buf())
    bEP[1].append(_Buf())
    _seed(bEP[0][2], bXS[0], bXS[1])
    _seed(bEP[1][2], bABT)
    _seed(bYEX[0], bQB)
    _seed(bYEX[1], bKB)
    _seed(bYEX[0], bKB)
    _seed(bYEX[1], bSQK)

    _seed(bYR[0][0], bCONST)
    _seed(bYR[0][1], bCONST)
    _seed(bYR[1][0], bXS[2])
    _seed(bYR[1][1], bXS[3])
    _seed(bDR[0][0], bQF)
    _seed(bDR[0][1], bQF)
    _seed(bDR[1][0], bSQ)
    _seed(bDR[1][1], bSQ)

    SPS = {0: (PSA, bPSA), 1: (PSB, bPSB)}
    YPS = {0: (PS4, bPS4), 1: (PS5, bPS5)}
    DPS = {0: (PS6, bPS6), 1: (PS7, bPS7)}

    pending = []

    def flush_pending():
        for eng, fn, reads, writes in pending:
            kb.op(eng, fn, reads=reads, writes=writes)
        pending.clear()

    for h in range(8):
        p = h // 2
        pbase = 64 * (h % 2)

        def emit_avden(pr, tc, h=h):
            st_last = 4 * (tc + 1) - 1
            for g in (0, 1):
                yps, byps = YPS[g]
                dps, bdps = DPS[g]
                ep, bep = EP[g][pr % 3], bEP[g][pr % 3]
                for k_ in (0, 1):
                    st = 2 * pr + k_
                    e = ep[:, k_ * 512:(k_ + 1) * 512]
                    vsl = VSB[:, st * 1024 + h * 128: st * 1024 + (h + 1) * 128]
                    kb.op("pe", lambda t, o=yps[:, :], l=vsl, e_=e,
                          s=(st == 0), en=(st == st_last):
                          t.matmul(o, l, e_, start=s, stop=en, tile_position=(0, 0)),
                          reads=[bV[st], bep], writes=[byps])
                    kb.op("pe", lambda t, o=dps[:, :], e_=e,
                          s=(st == 0), en=(st == st_last):
                          t.matmul(o, ONES[:, :], e_, start=s, stop=en,
                                   tile_position=(0, 0)),
                          reads=[bCONST, bep], writes=[bdps])

        for tc in range(NTC):
            npr = 2 * (tc + 1)
            for pr in range(npr):
                st0 = 2 * pr
                for g in (0, 1):
                    sps, bsps = SPS[g]
                    kslot, qslot = 8 + g * 4 + p, g * 4 + p
                    for k_ in (0, 1):
                        st = st0 + k_
                        lhs = QK[kslot][pbase:pbase + 64, st * 128:(st + 1) * 128]
                        rhs = QK[qslot][pbase:pbase + 64, tc * 512:(tc + 1) * 512]
                        kb.op("pe", lambda t, o=sps[:, k_ * 512:(k_ + 1) * 512],
                              l=lhs, r=rhs, tp=(pbase, 0):
                              t.matmul(o, l, r, start=True, stop=True, tile_position=tp),
                              reads=[bQK[kslot], bQK[qslot]], writes=[bsps])
                for g in (0, 1):
                    sps, bsps = SPS[g]
                    ep, bep = EP[g][pr % 3], bEP[g][pr % 3]
                    kb.op("act", lambda s, e=ep, ps=sps[:, :]:
                          s.activation(e, ps, AF.Exp, scale=0.125),
                          reads=[bsps], writes=[bep])
                for k_ in (0, 1):
                    st = st0 + k_
                    if st >= 4 * tc:
                        w = (st - 4 * tc) * 128
                        o = k_ * 512
                        for g in (0, 1):
                            ep, bep = EP[g][pr % 3], bEP[g][pr % 3]
                            if w > 0:
                                kb.op("dve", lambda v, e=ep[:, o:o + w]: v.memset(e, 0.0),
                                      writes=[bep])
                            kb.op("dve", lambda v, e=ep[:, o + w:o + w + 128]:
                                  v.tensor_mul(e, e, MSK[:, :]),
                                  reads=[bCONST], writes=[bep])
                for _ in range(2):
                    if pending:
                        p_eng, p_fn, p_reads, p_writes = pending.pop(0)
                        kb.op(p_eng, p_fn, reads=p_reads, writes=p_writes)
                if pr > 0:
                    emit_avden(pr - 1, tc)
            emit_avden(npr - 1, tc)

            par = tc % 2
            ysl, bysl = ycomb[h]
            dst = ysl[:, tc * 512:(tc + 1) * 512]
            for g in (0, 1):
                yps, byps = YPS[g]
                kb.op("dve", lambda v, d=YR[g][par][:, :], ps=yps[:, :]:
                      v.tensor_copy(d, ps), reads=[byps], writes=[bYR[g][par]])
            for g in (0, 1):
                dps, bdps = DPS[g]
                kb.op("dve", lambda v, d=DR[g][par][:, :], ps=dps[:, :]:
                      v.tensor_copy(d, ps), reads=[bdps], writes=[bDR[g][par]])
            for g in (0, 1):
                pending.append(("act",
                                (lambda s, dr=DR[g][par][:, :]:
                                 s.activation(USC[:, :], dr, AF.Ln)),
                                [bDR[g][par]], [bUSC]))
                pending.append(("act",
                                (lambda s: s.activation(BSB[:, :], USC[:, :],
                                                        AF.Exp, scale=-1.0)),
                                [bUSC], [bBSB]))
                if g == 0:
                    pending.append(("dve",
                                    (lambda v, d=dst, yr=YR[0][par][:, :]:
                                     v.tensor_mul(d, yr, BSB[:, :])),
                                    [bYR[0][par], bBSB], [bysl]))
                else:
                    pending.append(("dve",
                                    (lambda v: v.tensor_scalar(BSB[:, :], BSB[:, :],
                                                               LSC[:, 2:3], None, ALU.mult)),
                                    [bLSC, bBSB], [bBSB]))
                    pending.append(("dve",
                                    (lambda v, yr=YR[1][par][:, :]:
                                     v.tensor_mul(USC[:, :], yr, BSB[:, :])),
                                    [bYR[1][par], bBSB], [bUSC]))
                    pending.append(("dve",
                                    (lambda v, d=dst: v.tensor_add(d, d, USC[:, :])),
                                    [bUSC, bysl], [bysl]))

    flush_pending()

    if debug_stop == "attn":
        return _finish_debug(nc, kb, QK, VSB, YEX, bQK, bV, bYEX)

    for c in range(8):
        _seed(bZX[c], *(bV[4 * c:4 * c + 4] if c < 4
                        else [bQK[8 + 2 * (c - 4)], bQK[9 + 2 * (c - 4)]]))
    _x0src = [bDR[0][0], bDR[0][1], bDR[1][0], bDR[1][1],
              bABT, bABT, bXS[0], bXS[1]]
    for i in range(8):
        _seed(bX0[i], _x0src[i])
    _seed(bX0[4], bEP[1][2])
    _seed(bX0[5], bEP[1][2])
    _seed(bX0[6], bEP[0][2])
    _seed(bMT, bYR[0][0])
    _seed(bRT, bYR[0][1])
    _seed(bMT2, bYR[1][0])
    _seed(bRT2, bYR[1][1])

    def x0_load(tc):
        for c in range(8):
            kb.dma("sp", "dSP", lambda g, c=c, tc=tc: g.dma_start(
                out=X0T[c][:, :],
                in_=d_xT[c * 128:(c + 1) * 128, tc * 512:(tc + 1) * 512]),
                writes=[bX0[c]])

    x0_load(0)
    PRJ = {0: (PS4, bPS4), 1: (PS5, bPS5)}
    for tc in range(NTC):
        for oc in range(NDC):
            prj, bprj = PRJ[oc % 2]
            for kc in range(8):
                ysl, bysl = ycomb[kc]
                lhs = WP[:, kc * 1024 + oc * 128: kc * 1024 + (oc + 1) * 128]
                rhs = ysl[:, tc * 512:(tc + 1) * 512]
                kb.op("pe", lambda t, o=prj[:, :], l=lhs, r=rhs,
                      s=(kc == 0), e=(kc == 7):
                      t.matmul(o, l, r, start=s, stop=e, tile_position=(0, 0)),
                      reads=[bWc["wp"][kc // 2], bysl], writes=[bprj])
            kb.op("dve", lambda v, d=ZX[oc][:, tc * 512:(tc + 1) * 512],
                  ps=prj[:, :], x0=X0T[oc][:, :]: v.tensor_add(d, ps, x0),
                  reads=[bprj, bX0[oc]], writes=[bZX[oc]])
        if tc + 1 < NTC:
            x0_load(tc + 1)

    if debug_stop == "proj":
        return _finish_debug(nc, kb, QK, None, YEX, bQK, bV, bYEX, ZX=ZX, bZX=bZX, nqk=8)

    for f in range(32):
        _seed(bHT[f], ycomb[{0: 2, 1: 4, 2: 6, 3: 3, 4: 5, 5: 7, 6: 0, 7: 1}[f // 4]][1])
    for c in range(8):
        _seed(bX1B[c], bQK[3] if c < 4 else bQK[7])
    for q_ in range(4):
        _seed(bPQS[q_], bEP[1][q_ // 2])
        _seed(bZBS[q_], bEP[0][q_ // 2], bQNS[q_ // 2])

    def ln_block(tc, out_x1b, ps_s, bps_s, ps_q, bps_q, mt, bmt, rt, brt,
                 scr0, bscr0, scr1, bscr1):
        zb = (ZBS[0][:, :], ZBS[1][:, :])
        sq = (ZBS[2][:, :], ZBS[3][:, :])
        for oc in range(8):
            zsl = ZX[oc][:, tc * 512:(tc + 1) * 512]
            kb.op("dve", lambda v, d=zb[oc % 2], s=zsl: v.tensor_copy(d, s),
                  reads=[bZX[oc]], writes=[bZBS[oc % 2]])
            kb.op("act", lambda s_, d=sq[oc % 2], s=zsl: s_.activation(d, s, AF.Square),
                  reads=[bZX[oc]], writes=[bZBS[2 + oc % 2]])
            kb.op("pe", lambda t, o=ps_s[:, 0:512], r=zb[oc % 2], s=(oc == 0), e=(oc == 7):
                  t.matmul(o, ONES[:, :], r, start=s, stop=e, tile_position=(0, 0)),
                  reads=[bCONST, bZBS[oc % 2]], writes=[bps_s])
            kb.op("pe", lambda t, o=ps_q[:, 0:512], r=sq[oc % 2], s=(oc == 0), e=(oc == 7):
                  t.matmul(o, ONES[:, :], r, start=s, stop=e, tile_position=(0, 0)),
                  reads=[bCONST, bZBS[2 + oc % 2]], writes=[bps_q])
        kb.op("dve", lambda v: v.tensor_scalar(mt[:, :], ps_s[:, 0:512], 1.0 / 1024.0,
                                               None, ALU.mult),
              reads=[bps_s], writes=[bmt])
        kb.op("dve", lambda v: v.tensor_mul(scr0[:, :], mt[:, :], mt[:, :]),
              reads=[bmt], writes=[bscr0])
        kb.op("dve", lambda v: v.scalar_tensor_tensor(rt[:, :], ps_q[:, 0:512], 1.0 / 1024.0,
                                                      scr0[:, :], ALU.mult, ALU.subtract),
              reads=[bps_q, bscr0], writes=[brt])
        kb.op("act", lambda s: s.activation(scr0[:, :], rt[:, :], AF.Ln, bias=LSC[:, 4:5]),
              reads=[brt, bLSC], writes=[bscr0])
        kb.op("act", lambda s: s.activation(rt[:, :], scr0[:, :], AF.Exp, scale=-0.5),
              reads=[bscr0], writes=[brt])
        for oc in range(8):
            zsl = ZX[oc][:, tc * 512:(tc + 1) * 512]
            scr, bscr = (scr0, bscr0) if oc % 2 == 0 else (scr1, bscr1)
            kb.op("dve", lambda v, z=zsl, d=scr[:, :], m=mt[:, :]: v.tensor_sub(d, z, m),
                  reads=[bZX[oc], bmt], writes=[bscr])
            kb.op("dve", lambda v, z=zsl, d=scr[:, :], r=rt[:, :]: v.tensor_mul(z, d, r),
                  reads=[bscr, brt, bZX[oc]], writes=[bZX[oc]])
            if out_x1b:
                kb.op("pool", lambda g, d=X1B[oc][:, :], z=zsl: g.tensor_copy(d, z),
                      reads=[bZX[oc]], writes=[bX1B[oc]])

    ln_block(0, True, PS6, bPS6, PS7, bPS7, MT, bMT, RT, bRT,
             USC, bUSC, BSB, bBSB)
    for tc in range(NTC):
        for rc in range(2):
            for dc in range(8):
                lhs = U1S[:, dc * 256 + rc * 128: dc * 256 + (rc + 1) * 128]
                kb.op("pe", lambda t, o=PS4[:, :], l=lhs, r=X1B[dc][:, :],
                      s=(dc == 0), e=(dc == 7):
                      t.matmul(o, l, r, start=s, stop=e, tile_position=(0, 0)),
                      reads=[bW["u1"], bX1B[dc]], writes=[bPS4])
            kb.op("act", lambda s, d=PQS[:, rc * 512:(rc + 1) * 512], ps=PS4[:, :]:
                  s.copy(d, ps), reads=[bPS4], writes=[bPQS[rc]])
        for fc in range(32):
            hps, bhps = (PS5, bPS5) if fc % 2 == 0 else (PS4, bPS4)
            for rc in range(2):
                lhs = V1S[:, rc * 4096 + fc * 128: rc * 4096 + (fc + 1) * 128]
                rhs = PQS[:, rc * 512:(rc + 1) * 512]
                kb.op("pe", lambda t, o=hps[:, :], l=lhs, r=rhs,
                      s=(rc == 0), e=(rc == 1):
                      t.matmul(o, l, r, start=s, stop=e, tile_position=(0, 0)),
                      reads=[bW["v1"], bPQS[rc]], writes=[bhps])
            kb.op("act", lambda s, d=HT[fc][:, :], ps=hps[:, :], b=B1C[:, fc:fc + 1]:
                  s.activation(d, ps, AF.Gelu_apprx_tanh, bias=b),
                  reads=[bhps, bCONST], writes=[bHT[fc]])
        if tc + 1 < NTC:
            ln_block(tc + 1, True, PS6, bPS6, PS7, bPS7, MT, bMT, RT, bRT,
                     USC, bUSC, BSB, bBSB)
        for rc in range(2):
            for fc in range(32):
                lhs = U2S[:, fc * 256 + rc * 128: fc * 256 + (rc + 1) * 128]
                kb.op("pe", lambda t, o=PS7[:, :], l=lhs, r=HT[fc][:, :],
                      s=(fc == 0), e=(fc == 31):
                      t.matmul(o, l, r, start=s, stop=e, tile_position=(0, 0)),
                      reads=[bW["u2"], bHT[fc]], writes=[bPS7])
            kb.op("act", lambda s, d=PQS[:, 1024 + rc * 512: 1024 + (rc + 1) * 512],
                  ps=PS7[:, :]: s.copy(d, ps),
                  reads=[bPS7], writes=[bPQS[2 + rc]])
        for oc in range(NDC):
            ops_, bops = (PSA, bPSA) if oc % 2 == 0 else (PSB, bPSB)
            for rc in range(2):
                lhs = V2S[:, rc * 1024 + oc * 128: rc * 1024 + (oc + 1) * 128]
                rhs = PQS[:, 1024 + rc * 512: 1024 + (rc + 1) * 512]
                kb.op("pe", lambda t, o=ops_[:, 0:512], l=lhs, r=rhs,
                      s=(rc == 0), e=(rc == 1):
                      t.matmul(o, l, r, start=s, stop=e, tile_position=(0, 0)),
                      reads=[bW["v2"], bPQS[2 + rc]], writes=[bops])
            kb.op("dve", lambda v, d=ZX[oc][:, tc * 512:(tc + 1) * 512],
                  ps=ops_[:, 0:512], b=B2C[:, oc:oc + 1]:
                  v.scalar_tensor_tensor(d, ps, b, d, ALU.add, ALU.add),
                  reads=[bops, bCONST, bZX[oc]], writes=[bZX[oc]])
        ln_block(tc, False, PS5, bPS5, PS4, bPS4, MT2, bMT2, RT2, bRT2,
                 X0T[0], bX0[0], X0T[1], bX0[1])
        for oc in range(NDC):
            kb.dma("sp", "dSP", lambda g, oc=oc, tc=tc: g.dma_start(
                out=d_out[oc * 128:(oc + 1) * 128, tc * 512:(tc + 1) * 512],
                in_=ZX[oc][:, tc * 512:(tc + 1) * 512]),
                reads=[bZX[oc]])

    return _finish(nc, kb)


def _finish(nc, kb):
    kb.freeze()
    with nc.Block() as block:
        @block.sync
        def _(sync):
            kb.replay("sp", sync)
            for i in range(kb.npool["dSP"]):
                sync.wait_ge(kb.semh[f"dSP{i}"], kb.newtot[f"dSP{i}"])

        @block.tensor
        def _(tensor):
            kb.replay("pe", tensor)

        @block.scalar
        def _(scalar):
            kb.replay("act", scalar)

        @block.vector
        def _(vector):
            kb.replay("dve", vector)

        @block.gpsimd
        def _(gpsimd):
            kb.replay("pool", gpsimd)
    return nc


def _finish_debug(nc, kb, QK, VSB, YEX, bQK, bV, bYEX, ZX=None, bZX=None, nqk=16):
    d_dbg_qk = nc.dram_tensor("dbg_qk", [128, 16 * T], BF16, kind="ExternalOutput").ap()
    extra = []
    for b in bQK + bV + bYEX + (bZX or []):
        extra += list(b.w.items())
        extra += list(b.r.items())
    for i in range(nqk):
        kb.dma("sp", "dSP", lambda g, i=i: g.dma_start(
            out=d_dbg_qk[:, i * T:(i + 1) * T], in_=QK[i][:, :]), extra=extra)
    if VSB is not None:
        d_dbg_v = nc.dram_tensor("dbg_v", [128, NT * 1024], BF16, kind="ExternalOutput").ap()
        kb.dma("sp", "dSP", lambda g: g.dma_start(out=d_dbg_v, in_=VSB[:, :]), extra=extra)
    if YEX is not None:
        d_dbg_yex = nc.dram_tensor("dbg_yex", [128, 2 * T], BF16, kind="ExternalOutput").ap()
        for i in range(2):
            kb.dma("sp", "dSP", lambda g, i=i: g.dma_start(
                out=d_dbg_yex[:, i * T:(i + 1) * T], in_=YEX[i][:, :]), extra=extra)
    if ZX is not None:
        d_dbg_z = nc.dram_tensor("dbg_z", [128, 8 * T], F32, kind="ExternalOutput").ap()
        for i in range(8):
            kb.dma("sp", "dSP", lambda g, i=i: g.dma_start(
                out=d_dbg_z[:, i * T:(i + 1) * T], in_=ZX[i][:, :]), extra=extra)
    return _finish(nc, kb)


def _rotary_tables():
    import jax
    with jax.default_device(jax.devices("cpu")[0]):
        import jax.numpy as jnp
        inv_freq = 1.0 / 10000.0 ** (jnp.arange(0, 64, 2, dtype=jnp.float32) / 64)
        freqs = jnp.outer(jnp.arange(T, dtype=jnp.float32), inv_freq)
        return np.asarray(jnp.cos(freqs)), np.asarray(jnp.sin(freqs))


def stage_inputs(L0_attn_x, L0_attn_Wq, L0_attn_Wk, L0_attn_Wv, L0_attn_Wproj,
                 L0_attn_lambda_q1, L0_attn_lambda_k1, L0_attn_lambda_q2,
                 L0_attn_lambda_k2, L0_ffn_U1, L0_ffn_V1, L0_ffn_b1,
                 L0_ffn_U2, L0_ffn_V2, L0_ffn_b2):
    cos, sin = _rotary_tables()

    cos_t = cos.reshape(16, 128, 32).transpose(1, 0, 2)
    sin_t = sin.reshape(16, 128, 32).transpose(1, 0, 2)
    abt = np.concatenate([cos_t, cos_t, sin_t, -sin_t], axis=2)
    abt = np.ascontiguousarray(abt.reshape(128, 16 * 128)).astype(np.float32)

    s_i, t_i = np.arange(128)[:, None], np.arange(128)[None, :]
    lamr = np.concatenate([
        np.broadcast_to(np.asarray(v, np.float32)[None, :], (128, 32))
        for v in (L0_attn_lambda_q1, L0_attn_lambda_k1, L0_attn_lambda_q2, L0_attn_lambda_k2)
    ], axis=1)
    shared = {
        "wqT": np.ascontiguousarray(np.asarray(L0_attn_Wq, np.float32).T),
        "wkT": np.ascontiguousarray(np.asarray(L0_attn_Wk, np.float32).T),
        "wvT": np.ascontiguousarray(np.asarray(L0_attn_Wv, np.float32).T),
        "wpT": np.ascontiguousarray(np.asarray(L0_attn_Wproj, np.float32).T),
        "u1": np.ascontiguousarray(np.asarray(L0_ffn_U1, np.float32)),
        "v1f": np.ascontiguousarray(np.asarray(L0_ffn_V1, np.float32)),
        "u2": np.ascontiguousarray(np.asarray(L0_ffn_U2, np.float32)),
        "v2f": np.ascontiguousarray(np.asarray(L0_ffn_V2, np.float32)),
        "b1c": np.ascontiguousarray(np.asarray(L0_ffn_b1, np.float32).reshape(32, 128).T),
        "b2c": np.ascontiguousarray(np.asarray(L0_ffn_b2, np.float32).reshape(8, 128).T),
        "lamr": np.ascontiguousarray(lamr.astype(np.float32)),
        "abtT": abt,
        "maskb": np.ascontiguousarray((t_i >= s_i).astype(ml_dtypes.bfloat16)),
        "identb": np.ascontiguousarray(np.eye(128, dtype=ml_dtypes.bfloat16)),
    }
    x = np.asarray(L0_attn_x, np.float32)
    maps = []
    for b in range(8):
        xb = x[b]
        x1p = xb.reshape(16, 128, 8, 128).transpose(3, 0, 2, 1).reshape(128, -1)
        maps.append(dict(shared, xT=np.ascontiguousarray(xb.T),
                         x1p=np.ascontiguousarray(x1p)))
    return maps


def kernel(L0_attn_x, L0_attn_Wq, L0_attn_Wk, L0_attn_Wv, L0_attn_Wproj,
           L0_attn_lambda_q1, L0_attn_lambda_k1, L0_attn_lambda_q2,
           L0_attn_lambda_k2, L0_ffn_U1, L0_ffn_V1, L0_ffn_b1,
           L0_ffn_U2, L0_ffn_V2, L0_ffn_b2):
    in_maps = stage_inputs(L0_attn_x, L0_attn_Wq, L0_attn_Wk, L0_attn_Wv,
                           L0_attn_Wproj, L0_attn_lambda_q1, L0_attn_lambda_k1,
                           L0_attn_lambda_q2, L0_attn_lambda_k2, L0_ffn_U1,
                           L0_ffn_V1, L0_ffn_b1, L0_ffn_U2, L0_ffn_V2, L0_ffn_b2)
    nc = build_core()
    res = run_bass_kernel_spmd(nc, in_maps, core_ids=list(range(8)))
    out = np.stack([np.ascontiguousarray(r["outT"].T) for r in res.results])
    return out.astype(np.float32)
